# version 03
# baseline (speedup 1.0000x reference)
import numpy as np
import ml_dtypes
import concourse.bass as bass
import concourse.mybir as mybir
from concourse.bass_utils import run_bass_kernel_spmd

F32 = mybir.dt.float32
BF16 = mybir.dt.bfloat16
U8 = mybir.dt.uint8
AF = mybir.ActivationFunctionType
ALU = mybir.AluOpType
AX = mybir.AxisListType

NCORES = 8
B = 2
D = 1024
LLOC = 1024
TMAIN = B * LLOC
TPB = 1152
E = 4384
ZN = 2048
XBCN = 2304
DIN = 2048
NH = 32
HD = 64
CHUNK = 256
NCH = 4
NTC = 16
CCROWS = B * 128 + 2
BF = ml_dtypes.bfloat16


STREAMS = ("pe", "act", "dve", "pool", "sp")


class Op:
    __slots__ = ("stream", "fn", "kind", "deps", "targeted", "idx", "val",
                 "pre_nops", "waits", "name", "full128")

    def __init__(self, stream, fn, kind, name=""):
        self.stream = stream
        self.fn = fn
        self.kind = kind
        self.deps = []
        self.targeted = False
        self.idx = -1
        self.val = None
        self.pre_nops = 0
        self.waits = []
        self.name = name
        self.full128 = True


class Sched:
    def __init__(self, nc):
        self.nc = nc
        self.ops = {s: [] for s in STREAMS}
        self.writers = {}
        self.readers = {}
        self.barrier_ops = []
        self.dram_keys = set()

    def _edge(self, cons, prod, dram=False):
        if prod is cons:
            return
        if prod.stream == cons.stream:
            if prod.kind == "c" and cons.kind == "c":
                return
            if (prod.kind == "d" and cons.kind == "d"
                    and prod.full128 and cons.full128 and not dram):
                return
        cons.deps.append(prod)

    def add(self, stream, fn, reads=(), writes=(), scalar_reads=(), kind="c",
            name="", full128=True, extra_deps=()):
        op = Op(stream, fn, kind, name)
        op.full128 = full128
        my = len(self.ops[stream])
        for key in list(reads) + list(scalar_reads):
            dk = key in self.dram_keys
            for w in self.writers.get(key, ()):
                self._edge(op, w, dram=dk)
        for key in writes:
            dk = key in self.dram_keys
            for r in self.readers.get(key, ()):
                self._edge(op, r, dram=dk)
            for w in self.writers.get(key, ()):
                self._edge(op, w, dram=dk)
        for d in extra_deps:
            if d is not None:
                self._edge(op, d)
        for bo in self.barrier_ops:
            self._edge(op, bo)
        op.idx = my
        self.ops[stream].append(op)
        for key in list(reads) + list(scalar_reads):
            self.readers.setdefault(key, []).append(op)
        for key in writes:
            self.writers.setdefault(key, []).append(op)
            self.readers[key] = []
        return op

    def dma(self, stream, outap, inap, reads, writes, name="", full128=True,
            transpose=False):
        if transpose:
            fn = lambda eng, o=outap, i=inap: eng.dma_start_transpose(o, i)
        else:
            fn = lambda eng, o=outap, i=inap: eng.dma_start(out=o, in_=i)
        return self.add(stream, fn, reads=reads, writes=writes, kind="d",
                        name=name, full128=full128)

    def barrier(self):
        targets = []
        for s in STREAMS:
            if self.ops[s]:
                targets.append(self.ops[s][-1])
        self.barrier_ops = targets

    def checkpoint(self, stream):
        last_d = None
        for op in reversed(self.ops[stream]):
            if op.kind == "d":
                last_d = op
                break
        cp = self.add(stream, lambda eng: eng.nop(), kind="c",
                      name=f"cp_{stream}",
                      extra_deps=[last_d] if last_d else [])
        for key, ws in list(self.writers.items()):
            if any(w.stream == stream and w.kind == "d" for w in ws):
                keep = [w for w in ws if not (w.stream == stream and w.kind == "d")]
                self.writers[key] = keep + [cp]
        return cp

    def finalize_and_emit(self, ctx):
        nc = self.nc
        sems = {}
        for s in STREAMS:
            sems[(s, "c")] = ctx.enter_context(nc.semaphore(f"sem_{s}_c"))
            sems[(s, "d")] = ctx.enter_context(nc.semaphore(f"sem_{s}_d"))
        sems["cc"] = ctx.enter_context(nc.semaphore("sem_cc"))

        for s in STREAMS:
            for op in self.ops[s]:
                if op.kind == "d":
                    op.targeted = True
                for d in op.deps:
                    d.targeted = True
        finals = []
        for s in STREAMS:
            if self.ops[s]:
                last = self.ops[s][-1]
                last.targeted = True
                finals.append(last)
        fin = Op("sp", lambda eng: eng.nop(), "c", "final")
        fin.deps = list(finals)
        fin.idx = len(self.ops["sp"])
        self.ops["sp"].append(fin)

        for s in STREAMS:
            cc, cd = 0, 0
            for op in self.ops[s]:
                if op.kind == "c":
                    if op.targeted:
                        cc += 1
                        op.val = (sems[(s, "c")], cc)
                elif op.kind == "d":
                    if op.targeted:
                        cd += 1
                        op.val = (sems[(s, "d")], 16 * cd)
                else:
                    op.val = (sems["cc"], 1)
        for s in STREAMS:
            clock = {}
            for op in self.ops[s]:
                need = {}
                for d in op.deps:
                    assert d.val is not None, \
                        f"dep of {op.name} has no val: {d.name}"
                    sem, v = d.val
                    if need.get(sem, -1) < v:
                        need[sem] = v
                for sem, v in sorted(need.items(), key=lambda kv: str(kv[0])):
                    if clock.get(sem, 0) < v:
                        op.waits.append((sem, v))
                        clock[sem] = v

        block = ctx.enter_context(nc.Block())

        def emit(stream):
            def body(eng):
                for op in self.ops[stream]:
                    for sem, v in op.waits:
                        eng.wait_ge(sem, v)
                    for _ in range(op.pre_nops):
                        eng.nop()
                    inst = op.fn(eng) if op.fn is not None else None
                    if op.targeted and op.val is not None:
                        if inst is None:
                            inst = eng.nop()
                        sem, _ = op.val
                        amt = 16 if op.kind == "d" else 1
                        inst.then_inc(sem, amt)
            return body

        block.tensor(emit("pe"))
        block.scalar(emit("act"))
        block.vector(emit("dve"))
        block.gpsimd(emit("pool"))
        block.sync(emit("sp"))

    def stats(self):
        return {s: len(self.ops[s]) for s in STREAMS}


class Arena:
    def __init__(self, nc, ctx, nbytes, sch=None):
        self.sch = sch
        self.nc = nc
        self.nbytes = nbytes
        self.backing = ctx.enter_context(
            nc.sbuf_tensor("arena", [128, nbytes], U8))
        self.base = nc.lookup_mloc(self.backing).addr
        self.off = 0
        self.stack = []
        self.peak = 0

    def alloc(self, name, shape, dtype):
        per_part = 1
        for d_ in shape[1:]:
            per_part *= d_
        per_part *= mybir.dt.size(dtype)
        off = (self.off + 31) & ~31
        self.off = off + per_part
        self.peak = max(self.peak, self.off)
        assert self.off <= self.nbytes, \
            f"arena overflow at {name}: {self.off} > {self.nbytes}"
        h = self.nc.virtual_sbuf_tensor(name, shape, dtype)
        self.nc._tensor(h.name, h.shape, h.dtype, type="SB")
        mloc = self.nc.lookup_mloc(h)
        mloc.allocated = True
        mloc.addr = self.base + off
        return h

    def push(self):
        self.stack.append(self.off)

    def pop(self):
        self.off = self.stack.pop()
        if self.sch is not None:
            self.sch.barrier()


def ap3(t, pdim_rows, dims, offset=0):
    a = t[:]
    return bass.AP(a.tensor, a.offset + offset, [[a.ap[0][0], pdim_rows]] + dims)


def rep_inner(t, h_count, rep, col0=0, rows=128):
    a = t[:]
    return bass.AP(a.tensor, a.offset + col0,
                   [[a.ap[0][0], rows], [1, h_count], [0, rep]])


def rep_outer(t, h_count, rep, col0=0, rows=128):
    a = t[:]
    return bass.AP(a.tensor, a.offset + col0,
                   [[a.ap[0][0], rows], [0, rep], [1, h_count]])



LNCOL_DUMP = False


def build_program(stage=99, debug=False):
    from contextlib import ExitStack
    ctx = ExitStack()
    nc = bass.Bass("TRN2", target_bir_lowering=False)
    sch = Sched(nc)

    def din(name, shape, dtype=F32):
        return nc.dram_tensor(name, shape, dtype, kind="ExternalInput").ap()

    u = din("u", [B, LLOC, D])
    uh = din("uh", [B, 3, D])
    w_in_t = din("w_in_t", [D, E])
    conv_w_fm = din("conv_w_fm", [128, 72])
    conv_b_fm = din("conv_b_fm", [128, 18])
    dt_bias_row = din("dt_bias_row", [1, NH])
    a_log_row = din("a_log_row", [1, NH])
    d_param_row = din("d_param_row", [1, NH])
    norm_w_row = din("norm_w_row", [1, DIN])
    in_c_col = din("in_c_col", [128, 8])
    gbias_col = din("gbias_col", [128, 24])
    bo_row = din("bo_row", [1, D])
    ones_row = din("ones_row", [1, 128])
    consts_row = din("consts_row", [128, 4])
    tri_f32 = din("tri_f32", [128, 128])
    ones_sq = din("ones_sq", [128, 128])
    mask256 = din("mask256", [128, 256])
    m2t_ext = din("m2t_ext", [9, 8])
    w_out_t = din("w_out_t", [DIN, D])
    wffn = {}
    for nm in ("wf", "uf", "wi", "ui", "wc", "uc", "wo", "uo"):
        wffn[nm] = din(nm + "_t", [D, D])

    out = nc.dram_tensor("out", [B, LLOC, D], F32, kind="ExternalOutput").ap()

    xn_t = nc.dram_tensor("xn_t", [B * TPB, D], BF16).ap()
    xbc_f = nc.dram_tensor("xbc_f", [XBCN, TMAIN], BF16).ap()
    z_t = nc.dram_tensor("z_t", [TMAIN, ZN], BF16).ap()
    g_t = nc.dram_tensor("g_t", [TMAIN, DIN], BF16).ap()
    h_t = nc.dram_tensor("h_t", [TMAIN, D], F32).ap()
    prevsc_d = nc.dram_tensor("prevsc_d", [B * 3 * 128, NH * HD], BF16).ap()
    c2_d = nc.dram_tensor("c2_d", [D, LLOC], BF16).ap()
    cc_in_h = nc.dram_tensor("cc_in", [CCROWS, NH * HD], F32)
    cc_out_h = nc.dram_tensor("cc_out", [NCORES * CCROWS, NH * HD], F32)
    cc_in = cc_in_h.ap()
    cc_out = cc_out_h.ap()
    w_d = nc.dram_tensor("w_d", [8, 64], F32).ap()
    sch.dram_keys.update(["u", "uh", "xn_t", "xbc_f", "z_t", "g_t", "h_t",
                          "prevsc_d", "cc_in", "cc_out", "w_d", "c2_d", "out"])

    dbg = {}
    if debug:
        for nm, shape, dt_ in [
            ("dbg_xn", [B * TPB, D], BF16), ("dbg_xbc", [XBCN, TMAIN], BF16),
            ("dbg_z", [TMAIN, ZN], BF16), ("dbg_dt", [128, 512], F32),
            ("dbg_dacs", [128, 512], F32), ("dbg_decay", [128, 512], F32),
            ("dbg_ccin", [CCROWS, NH * HD], F32),
            ("dbg_ccout", [NCORES * CCROWS, NH * HD], F32),
            ("dbg_g", [TMAIN, DIN], BF16), ("dbg_h", [TMAIN, D], F32),
            ("dbg_prevsc", [B * 3 * 128, NH * HD], BF16),
            ("dbg_lncol", [NTC * 128, 32], F32),
            ("dbg_xtok", [2 * 128, 2176], BF16),
            ("dbg_st0", [128, 2048], F32),
            ("dbg_small", [128, 1024], F32),
            ("dbg_c2", [D, LLOC], BF16),
            ("dbg_cin", [2 * 128, NH * HD], F32),
            ("dbg_bias", [128, 24], F32),
            ("dbg_sdtx", [2 * 128, ZN], BF16),
        ]:
            dbg[nm] = nc.dram_tensor(nm, shape, dt_, kind="ExternalOutput").ap()

    pb = [ctx.enter_context(nc.psum_tensor(f"pb{i}", [128, 512], F32))
          for i in range(8)]
    PB = [f"pb{i}" for i in range(8)]

    AR = Arena(nc, ctx, 207 * 1024, sch)

    s_convw = AR.alloc("s_convw", [128, 72], F32)
    s_convb = AR.alloc("s_convb", [128, 18], F32)
    s_dtb_row = AR.alloc("s_dtb_row", [1, NH], F32)
    s_alog_row = AR.alloc("s_alog_row", [1, NH], F32)
    s_a_row = AR.alloc("s_a_row", [1, NH], F32)
    s_dp_row = AR.alloc("s_dp_row", [1, NH], F32)
    s_ones_row = AR.alloc("s_ones_row", [1, 128], F32)
    s_constrow = AR.alloc("s_constrow", [128, 4], F32)
    s_tri = AR.alloc("s_tri", [128, 128], F32)
    s_onessq = AR.alloc("s_onessq", [128, 128], F32)
    s_mask256 = AR.alloc("s_mask256", [128, 256], F32)
    s_incol = AR.alloc("s_incol", [128, 8], F32)
    s_incol_bf = AR.alloc("s_incol_bf", [128, 8], BF16)
    s_gbias = AR.alloc("s_gbias", [128, 24], F32)
    s_m2t = AR.alloc("s_m2t", [9, 8], F32)
    s_dtb_bc = AR.alloc("s_dtb_bc", [128, NH], F32)
    s_a_bc = AR.alloc("s_a_bc", [128, NH], F32)
    s_d_bc = AR.alloc("s_d_bc", [128, NH], F32)
    s_bo_bc = AR.alloc("s_bo_bc", [128, D], F32)
    s_Bfm = AR.alloc("s_Bfm", [128, TMAIN], BF16)
    s_Cfm = AR.alloc("s_Cfm", [128, TMAIN], BF16)
    s_gapA = AR.alloc("s_gapA", [128, 16], F32)
    s_gapB = AR.alloc("s_gapB", [128, 16], F32)

    def gap(stream):
        if stream == "act":
            sch.add("act", lambda eng: eng.copy(out=s_gapA[:], in_=s_gapB[:]),
                    name="gap_a")
        elif stream == "pool":
            sch.add("pool", lambda eng: eng.tensor_copy(s_gapA[:], s_gapB[:]),
                    name="gap_p")
        else:
            sch.add("dve", lambda eng: eng.tensor_copy(s_gapA[:], s_gapB[:]),
                    name="gap_v")

    s_dtsp = AR.alloc("s_dtsp", [128, 512], F32)
    s_da = AR.alloc("s_da", [128, 512], F32)
    s_dacs = AR.alloc("s_dacs", [128, 512], F32)
    s_decay = AR.alloc("s_decay", [128, 512], F32)
    s_rowsc = AR.alloc("s_rowsc", [128, 512], F32)
    s_cds = AR.alloc("s_cds", [128, 8 * NH], F32)
    s_dalast = AR.alloc("s_dalast", [128, 8 * NH], F32)
    s_dsumc = AR.alloc("s_dsumc", [1, 8 * NH], F32)
    s_dsum = AR.alloc("s_dsum", [1, 2 * NH], F32)
    s_biascol = AR.alloc("s_biascol", [128, 24], F32)

    sch.dma("sp", s_convw[:], conv_w_fm, ["conv_w_fm"], ["s_convw"], "ld_convw")
    sch.dma("sp", s_convb[:], conv_b_fm, ["conv_b_fm"], ["s_convb"], "ld_convb")
    sch.dma("sp", s_dtb_row[:], dt_bias_row, [], ["s_dtb_row"], "ld_dtb", full128=False)
    sch.dma("sp", s_alog_row[:], a_log_row, [], ["s_alog_row"], "ld_alog", full128=False)
    sch.dma("sp", s_dp_row[:], d_param_row, [], ["s_dp_row"], "ld_dp", full128=False)
    sch.dma("sp", s_ones_row[:], ones_row, [], ["s_ones_row"], "ld_ones", full128=False)
    sch.dma("sp", s_constrow[:], consts_row, [], ["s_constrow"], "ld_consts", full128=False)
    nc.const_aps.register(F32, 1e-5, s_constrow[:, 0:1])
    nc.const_aps.register(F32, 1.0, s_constrow[:, 1:2])
    nc.const_aps.register(F32, 0.0, s_constrow[:, 2:3])
    sch.dma("sp", s_tri[:], tri_f32, [], ["s_tri"], "ld_tri")
    sch.dma("sp", s_onessq[:], ones_sq, [], ["s_onessq"], "ld_onessq")
    sch.dma("sp", s_mask256[:], mask256, [], ["s_mask256"], "ld_mask")
    sch.dma("sp", s_incol[:], in_c_col, [], ["s_incol"], "ld_incol")
    sch.dma("sp", s_gbias[:], gbias_col, [], ["s_gbias"], "ld_gbias")
    sch.dma("sp", s_m2t[:], m2t_ext, [], ["s_m2t"], "ld_m2t", full128=False)
    AR.push()
    s_bo_row = AR.alloc("s_bo_row", [1, D], F32)
    sch.dma("sp", s_bo_row[:], bo_row, [], ["s_bo_row"], "ld_bo", full128=False)
    sch.checkpoint("sp")

    sch.add("act", lambda eng: eng.activation(out=s_a_row[:], in_=s_alog_row[:],
                                              func=AF.Exp),
            reads=["s_alog_row"], writes=["s_a_row"], name="a_exp")
    sch.add("act", lambda eng: eng.mul(out=s_a_row[:], in_=s_a_row[:], mul=-1.0),
            reads=["s_a_row"], writes=["s_a_row"], name="a_neg")
    sch.add("pool", lambda eng: eng.tensor_copy(s_incol_bf[:], s_incol[:]),
            reads=["s_incol"], writes=["s_incol_bf"], name="incbf")

    def bcast_row(srcap, srckey, dstap, dstkey, ncols, bank, name):
        p = pb[bank]
        sch.add("pe", lambda eng: eng.matmul(p[0:128, 0:ncols], s_ones_row[:],
                                             srcap, start=True, stop=True),
                reads=["s_ones_row", srckey], writes=[PB[bank]], name=f"bc_{name}")
        sch.add("dve", lambda eng: eng.tensor_copy(dstap, p[0:128, 0:ncols]),
                reads=[PB[bank]], writes=[dstkey], name=f"bce_{name}")

    bcast_row(s_dtb_row[0:1, :], "s_dtb_row", s_dtb_bc[:], "s_dtb_bc", NH, 0, "dtb")
    bcast_row(s_a_row[0:1, :], "s_a_row", s_a_bc[:], "s_a_bc", NH, 1, "abc")
    bcast_row(s_dp_row[0:1, :], "s_dp_row", s_d_bc[:], "s_d_bc", NH, 2, "dbc")
    for q in range(2):
        bcast_row(s_bo_row[0:1, q * 512:(q + 1) * 512], "s_bo_row",
                  s_bo_bc[:, q * 512:(q + 1) * 512], "s_bo_bc", 512, 3 + q,
                  f"bo{q}")
    AR.pop()

    env = dict(locals())
    if stage < 1:
        return _finish(nc, sch, ctx, AR, dbg, env)

    AR.push()
    xnfm = [AR.alloc(f"xnfm{d_}", [128, B * TPB], BF16) for d_ in range(8)]
    XN = [f"xnfm{d_}" for d_ in range(8)]

    AR.push()
    s_u = [AR.alloc(f"s_u{i}", [128, D], F32) for i in range(2)]
    s_xnst = [AR.alloc(f"s_xnst{i}", [128, D], BF16) for i in range(2)]
    s_uhalo = AR.alloc("s_uhalo", [6, D], F32)
    s_xnhalo = AR.alloc("s_xnhalo", [6, D], BF16)
    s_lncol = [AR.alloc(f"s_lncol{i}", [128, 32], F32) for i in range(3)]
    s_sqscr = [AR.alloc(f"s_sqscr{i}", [128, D], F32) for i in range(2)]
    s_xct = [AR.alloc(f"s_xct{i}", [128, D], F32) for i in range(2)]

    def ln_tile(utile, ukey, xout, xkey, rows, ci):
        c = s_lncol[ci]
        ck = f"s_lncol{ci}"
        sq = s_sqscr[ci % 2]
        sqk = f"s_sqscr{ci % 2}"
        cw = lambda j: c[0:rows, 4 * j:4 * j + 1]
        sch.add("act", lambda eng: eng.activation(
            out=sq[0:rows, :], in_=utile[0:rows, :], func=AF.Square),
            reads=[ukey], writes=[sqk], name="ln_sq")
        sch.add("dve", lambda eng: eng.tensor_reduce(
            cw(0), utile[0:rows, :], AX.X, ALU.add),
            reads=[ukey], writes=[ck], name="ln_s1")
        sch.add("dve", lambda eng: eng.tensor_reduce(
            cw(1), sq[0:rows, :], AX.X, ALU.add),
            reads=[sqk], writes=[ck + "q"], name="ln_ssq")
        sch.add("act", lambda eng: eng.mul(out=cw(2), in_=cw(0), mul=1.0 / D),
                reads=[ck], writes=[ck + "m"], name="ln_m")
        sch.add("dve", lambda eng: eng.tensor_mul(cw(3), cw(2), cw(2)),
                reads=[ck + "m"], writes=[ck + "m2"], name="ln_m2")
        gap("dve")
        sch.add("dve", lambda eng: eng.scalar_tensor_tensor(
            cw(4), cw(1), 1.0 / D, cw(3), ALU.mult, ALU.subtract),
            reads=[ck + "q", ck + "m2"], writes=[ck + "v"], name="ln_v")
        sch.add("act", lambda eng: eng.activation(
            out=cw(5), in_=cw(4), func=AF.Ln, bias=1e-5),
            reads=[ck + "v", "s_constrow"], writes=[ck + "l"], name="ln_ln")
        gap("act")
        sch.add("act", lambda eng: eng.activation(
            out=cw(6), in_=cw(5), func=AF.Exp, scale=-0.5),
            reads=[ck + "l"], writes=[ck + "r"], name="ln_rstd")
        xt_ = s_xct[ci % 2]
        xtk = f"s_xct{ci % 2}"
        sch.add("dve", lambda eng: eng.tensor_scalar_sub(
            xt_[0:rows, :], utile[0:rows, :], cw(2)),
            reads=[ukey], scalar_reads=[ck + "m"],
            writes=[xtk], name="ln_xc")
        sch.add("dve", lambda eng: eng.tensor_scalar_mul(
            xout[0:rows, :], xt_[0:rows, :], cw(6)),
            reads=[xtk], scalar_reads=[ck + "r"],
            writes=[xkey], name="ln_xn")

    sch.dma("sp", s_uhalo[:], uh.rearrange("b r d -> (b r) d"), ["uh"],
            ["s_uhalo"], "ld_uhalo", full128=False)
    sch.checkpoint("sp")
    ln_tile(s_uhalo, "s_uhalo", s_xnhalo, "s_xnhalo", 6, 2)
    for bb in range(B):
        sch.dma("act", xn_t[bb * TPB:bb * TPB + 3, :],
                s_xnhalo[bb * 3:(bb + 1) * 3, :], ["s_xnhalo"], ["xn_t"],
                f"st_xnh{bb}", full128=False)
    sch.checkpoint("act")

    s_lnstg = [AR.alloc(f"s_lnstg{i}", [128, 32], F32) for i in range(2)]
    for tc in range(NTC):
        bsel, r0, i = tc // 8, (tc % 8) * 128, tc % 2
        ut, xt = s_u[i], s_xnst[i]
        sch.dma("sp", ut[:], u[bsel, r0:r0 + 128, :], ["u"], [f"s_u{i}"],
                f"ld_u{tc}")
        ln_tile(ut, f"s_u{i}", xt, f"s_xnst{i}", 128, i)
        sch.dma("act", xn_t[bsel * TPB + 3 + r0:bsel * TPB + 3 + r0 + 128, :],
                xt[:], [f"s_xnst{i}"], ["xn_t"], f"st_xn{tc}")
        if debug and LNCOL_DUMP:
            ck = f"s_lncol{i}"
            sch.add("pool", lambda eng, j=i: eng.tensor_copy(
                s_lnstg[j][:], s_lncol[j][:]),
                reads=[ck, ck + "q", ck + "m", ck + "m2", ck + "v", ck + "l",
                       ck + "r"],
                writes=[f"s_lnstg{i}"], name=f"lnstg{tc}")
            sch.dma("sp", dbg["dbg_lncol"][tc * 128:(tc + 1) * 128, :],
                    s_lnstg[i][:], [f"s_lnstg{i}"], ["dbg_lncol"],
                    f"st_lncol{tc}")

    for dc in range(8):
        inap = bass.AP(xn_t.tensor, dc * 128, [[D, B * TPB], [1, 128]])
        sch.dma("sp", xnfm[dc][:], inap, ["xn_t"], [XN[dc]], f"tp_xn{dc}",
                transpose=True)
    sch.checkpoint("sp")
    AR.pop()

    env = dict(locals())
    if stage < 2:
        return _finish(nc, sch, ctx, AR, dbg, env)

    AR.push()
    win = [AR.alloc(f"win{d_}", [128, E], BF16) for d_ in range(8)]
    WIN = [f"win{d_}" for d_ in range(8)]
    for dc in range(8):
        sch.dma("pool", win[dc][:], w_in_t[dc * 128:(dc + 1) * 128, :],
                ["w_in_t"], [WIN[dc]], f"ld_win{dc}")
    sch.checkpoint("pool")
    s_xpre = [AR.alloc(f"s_xpre{i}", [128, 2054], F32) for i in range(2)]
    s_convt = [AR.alloc(f"s_convt{i}", [128, TMAIN], F32) for i in range(2)]
    s_xbcs = [AR.alloc(f"s_xbcs{i}", [128, TMAIN], BF16) for i in range(2)]

    def tcol(tc):
        return (tc // 8) * TPB + 3 + (tc % 8) * 128

    for ec in range(18):
        i = ec % 2
        xp, xpk = s_xpre[i], f"s_xpre{i}"
        hbank = 2 + i
        for tq in range(4):
            bsel, q = tq // 2, tq % 2
            bank = tq % 2
            for dc in range(8):
                lhsT = win[dc][:, 2048 + ec * 128:2048 + (ec + 1) * 128]
                rhs = xnfm[dc][:, bsel * TPB + 3 + q * 512:
                               bsel * TPB + 3 + (q + 1) * 512]
                sch.add("pe", lambda eng, p=pb[bank], l=lhsT, r=rhs,
                        st=(dc == 0), sp_=(dc == 7): eng.matmul(
                            p[:], l, r, start=st, stop=sp_),
                        reads=[WIN[dc], XN[dc]], writes=[PB[bank]],
                        name=f"ip{ec}_{tq}_{dc}")
                if tq == 0:
                    hap = ap3(xnfm[dc], 128, [[TPB, 2], [1, 3]])
                    sch.add("pe", lambda eng, p=pb[hbank], l=lhsT, r=hap,
                            st=(dc == 0), sp_=(dc == 7): eng.matmul(
                                p[0:128, 0:6], l, r, start=st, stop=sp_),
                            reads=[WIN[dc], XN[dc]], writes=[PB[hbank]],
                            name=f"iph{ec}_{dc}")
            dst = xp[:, bsel * 1027 + 3 + q * 512:bsel * 1027 + 3 + (q + 1) * 512]
            sch.add("act", lambda eng, d=dst, p=pb[bank]: eng.copy(out=d, in_=p[:]),
                    reads=[PB[bank]], writes=[xpk], name=f"ipe{ec}_{tq}")
        hdst = ap3(xp, 128, [[1027, 2], [1, 3]])
        sch.add("act", lambda eng, d=hdst, p=pb[hbank]: eng.copy(
            out=d, in_=p[0:128, 0:6]),
            reads=[PB[hbank]], writes=[xpk], name=f"ipeh{ec}")
        ct, ctk = s_convt[i], f"s_convt{i}"
        eng_sel = "dve"
        for bsel in range(B):
            od = ct[:, bsel * 1024:(bsel + 1) * 1024]
            for k in range(4):
                src = xp[:, bsel * 1027 + k:bsel * 1027 + k + 1024]
                wc = s_convw[:, ec * 4 + k:ec * 4 + k + 1]
                if k == 0:
                    sch.add(eng_sel, lambda eng, o=od, s=src, w=wc:
                            eng.tensor_scalar_mul(o, s, w),
                            reads=[xpk], scalar_reads=["s_convw"], writes=[ctk],
                            name=f"cv{ec}_{bsel}_0")
                else:
                    sch.add(eng_sel, lambda eng, o=od, s=src, w=wc:
                            eng.scalar_tensor_tensor(o, s, w, o, ALU.mult, ALU.add),
                            reads=[xpk, ctk], scalar_reads=["s_convw"],
                            writes=[ctk], name=f"cv{ec}_{bsel}_{k}")
        if ec == 16:
            sil_out, sil_key = s_Bfm, "s_Bfm"
        elif ec == 17:
            sil_out, sil_key = s_Cfm, "s_Cfm"
        else:
            sil_out, sil_key = s_xbcs[i], f"s_xbcs{i}"
        sch.add("act", lambda eng, o=sil_out, s=ct, b_=s_convb[:, ec:ec + 1]:
                eng.activation(out=o[:], in_=s[:], func=AF.Silu, bias=b_),
                reads=[ctk], scalar_reads=["s_convb"], writes=[sil_key],
                name=f"silu{ec}")
        if ec < 17:
            sch.dma("act", xbc_f[ec * 128:(ec + 1) * 128, :], sil_out[:],
                    [sil_key], ["xbc_f"], f"st_xbc{ec}")

    s_zst = [AR.alloc(f"s_zst{i}", [128, ZN], BF16) for i in range(2)]
    for tc in range(NTC):
        i = tc % 2
        qb = 0 if i == 0 else 4
        for dc in range(8):
            lhsT = xnfm[dc][:, tcol(tc):tcol(tc) + 128]
            for q in range(4):
                rhs = win[dc][:, q * 512:(q + 1) * 512]
                sch.add("pe", lambda eng, p=pb[qb + q], l=lhsT, r=rhs,
                        st=(dc == 0), sp_=(dc == 7): eng.matmul(
                            p[:], l, r, start=st, stop=sp_),
                        reads=[WIN[dc], XN[dc]], writes=[PB[qb + q]],
                        name=f"z{tc}_{dc}_{q}")
        for q in range(4):
            dst = s_zst[i][:, q * 512:(q + 1) * 512]
            sch.add("act", lambda eng, d=dst, p=pb[qb + q]: eng.activation(
                out=d, in_=p[:], func=AF.Silu),
                reads=[PB[qb + q]], writes=[f"s_zst{i}"], name=f"ze{tc}_{q}")
        sch.dma("act", z_t[tc * 128:(tc + 1) * 128, :], s_zst[i][:],
                [f"s_zst{i}"], ["z_t"], f"st_z{tc}")
    for tc in range(NTC):
        bank = tc % 2
        for dc in range(8):
            lhsT = xnfm[dc][:, tcol(tc):tcol(tc) + 128]
            rhs = win[dc][:, 4352:4384]
            sch.add("pe", lambda eng, p=pb[bank], l=lhsT, r=rhs,
                    st=(dc == 0), sp_=(dc == 7): eng.matmul(
                        p[0:128, 0:NH], l, r, start=st, stop=sp_),
                    reads=[WIN[dc], XN[dc]], writes=[PB[bank]],
                    name=f"dt{tc}_{dc}")
        sch.add("dve", lambda eng, t=tc, p=pb[bank]: eng.tensor_copy(
            s_dtsp[:, t * NH:(t + 1) * NH], p[0:128, 0:NH]),
            reads=[PB[bank]], writes=["s_dtsp"], name=f"dte{tc}")
    AR.pop()

    AR.push()
    s_dtA = AR.alloc("s_dtA", [128, 512], F32)
    s_dtB = AR.alloc("s_dtB", [128, 512], F32)
    sch.add("dve", lambda eng: eng.tensor_tensor(
        s_dtsp[:], s_dtsp[:], rep_outer(s_dtb_bc, NH, NTC), ALU.add),
        reads=["s_dtsp", "s_dtb_bc"], writes=["s_dtsp"], name="dt_add")
    sch.add("act", lambda eng: eng.activation(out=s_dtA[:], in_=s_dtsp[:],
                                              func=AF.Abs),
            reads=["s_dtsp"], writes=["s_dtA"], name="dt_abs")
    sch.add("act", lambda eng: eng.activation(out=s_dtA[:], in_=s_dtA[:],
                                              func=AF.Exp, scale=-1.0),
            reads=["s_dtA"], writes=["s_dtA"], name="dt_exp")
    sch.add("act", lambda eng: eng.activation(out=s_dtA[:], in_=s_dtA[:],
                                              func=AF.Ln, bias=1.0),
            reads=["s_dtA", "s_constrow"], writes=["s_dtA"], name="dt_ln")
    sch.add("dve", lambda eng: eng.tensor_scalar_max(s_dtB[:], s_dtsp[:], 0.0),
            reads=["s_dtsp"], writes=["s_dtB"], name="dt_max")
    sch.add("dve", lambda eng: eng.tensor_add(s_dtsp[:], s_dtA[:], s_dtB[:]),
            reads=["s_dtA", "s_dtB"], writes=["s_dtsp"], name="dt_sp")
    sch.add("dve", lambda eng: eng.tensor_tensor(
        s_da[:], s_dtsp[:], rep_outer(s_a_bc, NH, NTC), ALU.mult),
        reads=["s_dtsp", "s_a_bc"], writes=["s_da"], name="dt_da")
    for bc in range(8):
        t0, t1 = 2 * bc, 2 * bc + 1
        b0 = pb[bc % 2]
        b1 = pb[2 + bc % 2]
        br = pb[4 + bc % 2]
        bl = pb[6 + bc % 2]
        daf = lambda t: s_da[:, t * NH:(t + 1) * NH]
        sch.add("pe", lambda eng, p=b0, r=daf(t0): eng.matmul(
            p[0:128, 0:NH], s_tri[:], r, start=True, stop=True),
            reads=["s_tri", "s_da"], writes=[PB[bc % 2]], name=f"cs0_{bc}")
        sch.add("pe", lambda eng, p=b1, r=daf(t0): eng.matmul(
            p[0:128, 0:NH], s_onessq[:], r, start=True, stop=False),
            reads=["s_onessq", "s_da"], writes=[PB[2 + bc % 2]], name=f"cs1a_{bc}")
        sch.add("pe", lambda eng, p=b1, r=daf(t1): eng.matmul(
            p[0:128, 0:NH], s_tri[:], r, start=False, stop=True),
            reads=["s_tri", "s_da"], writes=[PB[2 + bc % 2]], name=f"cs1b_{bc}")
        sch.add("pe", lambda eng, p=br, r=daf(t0): eng.matmul(
            p[0:1, 0:NH], s_onessq[:, 0:1], r, start=True, stop=False),
            reads=["s_onessq", "s_da"], writes=[PB[4 + bc % 2]], name=f"csr_{bc}")
        sch.add("pe", lambda eng, p=br, r=daf(t1): eng.matmul(
            p[0:1, 0:NH], s_onessq[:, 0:1], r, start=False, stop=True),
            reads=["s_onessq", "s_da"], writes=[PB[4 + bc % 2]], name=f"csr2_{bc}")
        sch.add("dve", lambda eng, p=b0, t=t0: eng.tensor_copy(
            s_dacs[:, t * NH:(t + 1) * NH], p[0:128, 0:NH]),
            reads=[PB[bc % 2]], writes=["s_dacs"], name=f"cse0_{bc}")
        sch.add("dve", lambda eng, p=b1, t=t1: eng.tensor_copy(
            s_dacs[:, t * NH:(t + 1) * NH], p[0:128, 0:NH]),
            reads=[PB[2 + bc % 2]], writes=["s_dacs"], name=f"cse1_{bc}")
        sch.add("dve", lambda eng, p=br, k=bc: eng.tensor_copy(
            s_dsumc[0:1, k * NH:(k + 1) * NH], p[0:1, 0:NH]),
            reads=[PB[4 + bc % 2]], writes=["s_dsumc"], name=f"cser_{bc}")
        sch.add("pe", lambda eng, p=bl, k=bc: eng.matmul(
            p[0:128, 0:NH], s_ones_row[:],
            s_dsumc[0:1, k * NH:(k + 1) * NH], start=True, stop=True),
            reads=["s_ones_row", "s_dsumc"], writes=[PB[6 + bc % 2]],
            name=f"csl_{bc}")
        sch.add("dve", lambda eng, p=bl, k=bc: eng.tensor_copy(
            s_dalast[:, k * NH:(k + 1) * NH], p[0:128, 0:NH]),
            reads=[PB[6 + bc % 2]], writes=["s_dalast"], name=f"csle_{bc}")
        sch.add("dve", lambda eng, t=t0, k=bc: eng.tensor_tensor(
            s_rowsc[:, t * NH:(t + 2) * NH], s_dacs[:, t * NH:(t + 2) * NH],
            rep_outer(s_dalast, NH, 2, col0=k * NH), ALU.subtract),
            reads=["s_dacs", "s_dalast"], writes=["s_rowsc"], name=f"df_{bc}")
        sch.add("act", lambda eng, t=t0: eng.activation(
            out=s_decay[:, t * NH:(t + 2) * NH],
            in_=s_rowsc[:, t * NH:(t + 2) * NH], func=AF.Exp, scale=-1.0),
            reads=["s_rowsc"], writes=["s_decay"], name=f"dk_{bc}")
        sch.add("act", lambda eng, t=t0: eng.activation(
            out=s_rowsc[:, t * NH:(t + 2) * NH],
            in_=s_rowsc[:, t * NH:(t + 2) * NH], func=AF.Exp),
            reads=["s_rowsc"], writes=["s_rowsc"], name=f"rs_{bc}")
        sch.add("act", lambda eng, k=bc: eng.activation(
            out=s_cds[:, k * NH:(k + 1) * NH],
            in_=s_dalast[:, k * NH:(k + 1) * NH], func=AF.Exp),
            reads=["s_dalast"], writes=["s_cds"], name=f"cd_{bc}")
    for bsel in range(B):
        c0 = bsel * 4
        for tcn in range(8):
            sch.add("pe", lambda eng, p=pb[bsel], t=c0 * 2 + tcn,
                    st=(tcn == 0), sp_=(tcn == 7): eng.matmul(
                        p[0:1, 0:NH], s_onessq[:, 0:1],
                        s_da[:, t * NH:(t + 1) * NH], start=st, stop=sp_),
                    reads=["s_onessq", "s_da"], writes=[PB[bsel]],
                    name=f"dsm{bsel}_{tcn}")
        sch.add("act", lambda eng, p=pb[bsel], b_=bsel: eng.copy(
            out=s_dsum[0:1, b_ * NH:(b_ + 1) * NH], in_=p[0:1, 0:NH]),
            reads=[PB[bsel]], writes=["s_dsum"], name=f"dse{bsel}")
    AR.pop()

    env = dict(locals())
    if stage < 3:
        return _finish(nc, sch, ctx, AR, dbg, env)

    AR.push()
    s_guardA2 = AR.alloc("s_guardA2", [128, 1024], F32)
    s_xtok = [AR.alloc(f"s_xtok{i}", [128, 2176], BF16) for i in range(2)]
    s_sdtx = [AR.alloc(f"s_sdtx{i}", [128, ZN], BF16) for i in range(2)]
    s_carry = [AR.alloc(f"s_carry{b_}", [128, 2048], F32) for b_ in range(B)]
    s_t1 = AR.alloc("s_t1", [128, 2048], F32)
    s_prevst = [AR.alloc(f"s_prevst{i}", [128, 2048], BF16) for i in range(2)]
    s_sc1 = AR.alloc("s_sc1", [128, 64], F32)

    def load_xtok(tc, i, tag):
        inap = bass.AP(xbc_f.tensor, tc * 128, [[TMAIN, 2176], [1, 128]])
        sch.dma("sp", s_xtok[i][:], inap, ["xbc_f"], [f"s_xtok{i}"],
                f"tpx{tag}", transpose=True)

    def make_sdtx(tc, i, xl=s_xtok, sl=s_sdtx, cl=s_sc1):
        sch.add("dve", lambda eng, t=tc, j=i: eng.tensor_mul(
            cl[:, j * NH:(j + 1) * NH], s_dtsp[:, t * NH:(t + 1) * NH],
            s_decay[:, t * NH:(t + 1) * NH]),
            reads=["s_dtsp", "s_decay"], writes=["s_sc1"], name=f"sc{tc}")
        sch.add("dve", lambda eng, j=i: eng.tensor_tensor(
            sl[j][:], xl[j][:, 0:2048],
            rep_inner(cl, NH, HD, col0=j * NH), ALU.mult),
            reads=[f"s_xtok{i}", "s_sc1"], writes=[f"s_sdtx{i}"],
            name=f"sdtx{tc}")

    for bsel in range(B):
        for c in range(NCH):
            bc = bsel * NCH + c
            tcs = (2 * bc, 2 * bc + 1)
            for j, tc in enumerate(tcs):
                load_xtok(tc, j, f"p2_{tc}")
            sch.checkpoint("sp")
            for j, tc in enumerate(tcs):
                make_sdtx(tc, j)
            if debug and bc == 0:
                for j in range(2):
                    sch.dma("sp", dbg["dbg_xtok"][j * 128:(j + 1) * 128, :],
                            s_xtok[j][:], [f"s_xtok{j}"], ["dbg_xtok"],
                            f"dbgxt{j}")
                    sch.dma("sp", dbg["dbg_sdtx"][j * 128:(j + 1) * 128, :],
                            s_sdtx[j][:], [f"s_sdtx{j}"], ["dbg_sdtx"],
                            f"dbgsd{j}")
            qb = 0 if bc % 2 == 0 else 4
            for jc in range(2):
                for q in range(4):
                    sch.add("pe", lambda eng, p=pb[qb + q], j=jc, q_=q,
                            xl=s_xtok, sl=s_sdtx:
                            eng.matmul(p[:], xl[j][:, 2048:2176],
                                       sl[j][:, q_ * 512:(q_ + 1) * 512],
                                       start=(j == 0), stop=(j == 1)),
                            reads=[f"s_xtok{jc}", f"s_sdtx{jc}"],
                            writes=[PB[qb + q]], name=f"st{bc}_{jc}_{q}")
            ck, cn = s_carry[bsel], f"s_carry{bsel}"
            if c == 0:
                for q in range(4):
                    sch.add("dve", lambda eng, p=pb[qb + q], q_=q, c_=ck:
                            eng.tensor_copy(c_[:, q_ * 512:(q_ + 1) * 512], p[:]),
                            reads=[PB[qb + q]], writes=[cn], name=f"sc0e{bc}_{q}")
                if debug and bc == 0:
                    sch.dma("sp", dbg["dbg_st0"], ck[:], [cn], ["dbg_st0"],
                            "dbgst0")
            else:
                sch.add("dve", lambda eng, k=bc, c_=ck: eng.tensor_tensor(
                    s_t1[:], c_[:], rep_inner(s_cds, NH, HD, col0=k * NH),
                    ALU.mult),
                    reads=[cn, "s_cds"], writes=["s_t1"], name=f"t1_{bc}")
                i = bc % 2
                sch.add("pool", lambda eng, j=i: eng.tensor_copy(
                    s_prevst[j][:], s_t1[:]),
                    reads=["s_t1"], writes=[f"s_prevst{i}"], name=f"pvc{bc}")
                sch.dma("sp", prevsc_d[(bsel * 3 + c - 1) * 128:
                                       (bsel * 3 + c) * 128, :],
                        s_prevst[i][:], [f"s_prevst{i}"], ["prevsc_d"],
                        f"st_pv{bc}")
                for q in range(4):
                    sch.add("dve", lambda eng, p=pb[qb + q], q_=q, c_=ck:
                            eng.tensor_add(c_[:, q_ * 512:(q_ + 1) * 512],
                                           s_t1[:, q_ * 512:(q_ + 1) * 512],
                                           p[:]),
                            reads=[PB[qb + q], "s_t1"], writes=[cn],
                            name=f"scu{bc}_{q}")
        sch.dma("sp", cc_in[bsel * 128:(bsel + 1) * 128, :], s_carry[bsel][:],
                [f"s_carry{bsel}"], ["cc_in"], f"st_S{bsel}")
    for bsel in range(B):
        sch.dma("sp", cc_in[B * 128 + bsel:B * 128 + bsel + 1, 0:NH],
                s_dsum[0:1, bsel * NH:(bsel + 1) * NH], ["s_dsum"], ["cc_in"],
                f"st_ds{bsel}", full128=False)
    sch.checkpoint("sp")
    sch.add("pool", lambda eng: eng.collective_compute(
        "AllGather", ALU.bypass, replica_groups=[list(range(NCORES))],
        ins=[cc_in_h.ap().opt()], outs=[cc_out_h.ap().opt()]),
        reads=["cc_in"], writes=["cc_out"], kind="cc", name="allgather")
    AR.pop()

    env = dict(locals())
    if stage < 4:
        return _finish(nc, sch, ctx, AR, dbg, env)

    AR.push()
    AR.push()
    s_wg = {}
    for g in ("wf", "wi", "wc"):
        s_wg[g] = [AR.alloc(f"s_{g}{d_}", [128, D], BF16) for d_ in range(8)]
        for dc in range(8):
            sch.dma("pool", s_wg[g][dc][:], wffn[g][dc * 128:(dc + 1) * 128, :],
                    [g + "_t"], [f"s_{g}{dc}"], f"ld_{g}{dc}")
        sch.checkpoint("pool")
    s_ug = [AR.alloc(f"s_ug{d_}", [128, D], BF16) for d_ in range(8)]
    s_ug2 = [AR.alloc(f"s_ug2{d_}", [128, D], BF16) for d_ in range(8)]
    s_rrow = AR.alloc("s_rrow", [1, D], F32)
    s_c1 = [AR.alloc(f"s_c1_{m}", [128, 512], F32) for m in range(8)]
    s_c1b = [AR.alloc(f"s_c1b{m}", [128, 512], BF16) for m in range(8)]
    s_c2a = [AR.alloc(f"s_c2a{m}", [128, 512], F32) for m in range(8)]
    s_c2st = AR.alloc("s_c2st", [128, 512], BF16)
    s_gt = [AR.alloc(f"s_gt{i}", [128, 512], F32) for i in range(2)]

    for gi, g in enumerate(("uf", "ui", "uc")):
        for cc_ in range(8):
            sch.dma("pool", s_ug[cc_][:], wffn[g][cc_ * 128:(cc_ + 1) * 128, :],
                    [g + "_t"], [f"s_ug{cc_}"], f"mv_ld_{g}{cc_}")
        sch.checkpoint("pool")
        for hv in range(2):
            for cc_ in range(8):
                sch.add("pe", lambda eng, p=pb[hv], c_=cc_, h_=hv: eng.matmul(
                    p[0:1, 0:512], s_incol_bf[:, c_:c_ + 1],
                    s_ug[c_][:, h_ * 512:(h_ + 1) * 512],
                    start=(c_ == 0), stop=(c_ == 7)),
                    reads=["s_incol_bf", f"s_ug{cc_}"], writes=[PB[hv]],
                    name=f"mv{g}_{hv}_{cc_}")
            sch.add("dve", lambda eng, p=pb[hv], h_=hv: eng.tensor_copy(
                s_rrow[0:1, h_ * 512:(h_ + 1) * 512], p[0:1, 0:512]),
                reads=[PB[hv]], writes=["s_rrow"], name=f"mve{g}_{hv}")
        for mc in range(8):
            bank = 2 + mc % 2
            sch.add("pe", lambda eng, p=pb[bank], m=mc: eng.matmul(
                p[0:128, 0:1], s_rrow[0:1, m * 128:(m + 1) * 128],
                s_ones_row[0:1, 0:1], start=True, stop=True),
                reads=["s_rrow", "s_ones_row"], writes=[PB[bank]],
                name=f"mvc{g}_{mc}")
            col = gi * 8 + mc
            sch.add("dve", lambda eng, p=pb[bank], c_=col: eng.tensor_add(
                s_biascol[:, c_:c_ + 1], p[0:128, 0:1], s_gbias[:, c_:c_ + 1]),
                reads=[PB[bank], "s_gbias"], writes=["s_biascol"],
                name=f"mvb{g}_{mc}")

    GSIG = {"wf": (AF.Sigmoid, 0), "wi": (AF.Sigmoid, 1), "wc": (AF.Tanh, 2)}
    for lh in range(2):
        xcol0 = 3 + lh * 512
        for mc in range(8):
            banks = {"wf": 0, "wi": 1, "wc": 2}
            if mc % 2:
                banks = {k: v + 4 for k, v in banks.items()}
            for g in ("wf", "wi", "wc"):
                bk = banks[g]
                for dc in range(8):
                    sch.add("pe", lambda eng, p=pb[bk], g_=g, d_=dc, m=mc,
                            x0=xcol0:
                            eng.matmul(p[:], s_wg[g_][d_][:, m * 128:(m + 1) * 128],
                                       xnfm[d_][:, x0:x0 + 512],
                                       start=(d_ == 0), stop=(d_ == 7)),
                            reads=[f"s_{g}{dc}", XN[dc]], writes=[PB[bk]],
                            name=f"s1{g}{lh}_{mc}_{dc}")
            func_f, _ = GSIG["wf"]
            sch.add("act", lambda eng, p=pb[banks['wf']], m=mc: eng.activation(
                out=s_gt[0][:], in_=p[:], func=AF.Sigmoid,
                bias=s_biascol[:, 0 + m:0 + m + 1]),
                reads=[PB[banks["wf"]]], scalar_reads=["s_biascol"],
                writes=["s_gt0"], name=f"s1ef{lh}_{mc}")
            sch.add("act", lambda eng, p=pb[banks['wi']], m=mc: eng.activation(
                out=s_gt[1][:], in_=p[:], func=AF.Sigmoid,
                bias=s_biascol[:, 8 + m:8 + m + 1]),
                reads=[PB[banks["wi"]]], scalar_reads=["s_biascol"],
                writes=["s_gt1"], name=f"s1ei{lh}_{mc}")
            sch.add("act", lambda eng, p=pb[banks['wc']], m=mc: eng.activation(
                out=s_c1[m][:], in_=p[:], func=AF.Tanh,
                bias=s_biascol[:, 16 + m:16 + m + 1]),
                reads=[PB[banks["wc"]]], scalar_reads=["s_biascol"],
                writes=[f"s_c1_{mc}"], name=f"s1ec{lh}_{mc}")
            sch.add("dve", lambda eng, m=mc: eng.tensor_mul(
                s_c1[m][:], s_gt[1][:], s_c1[m][:]),
                reads=["s_gt1", f"s_c1_{mc}"], writes=[f"s_c1_{mc}"],
                name=f"s1m{lh}_{mc}")
            sch.add("dve", lambda eng, m=mc: eng.scalar_tensor_tensor(
                s_c1[m][:], s_gt[0][:], s_incol[:, m:m + 1], s_c1[m][:],
                ALU.mult, ALU.add),
                reads=["s_gt0", f"s_c1_{mc}"], scalar_reads=["s_incol"],
                writes=[f"s_c1_{mc}"], name=f"s1c{lh}_{mc}")
            sch.add("pool", lambda eng, m=mc: eng.tensor_copy(
                s_c1b[m][:], s_c1[m][:]),
                reads=[f"s_c1_{mc}"], writes=[f"s_c1b{mc}"], name=f"s1b{lh}_{mc}")
        for gsl, g in ((s_ug, "uc"), (s_ug2, "ui")):
            for cc_ in range(8):
                key = "s_ug" if gsl is s_ug else "s_ug2"
                sch.dma("pool", gsl[cc_][:], wffn[g][cc_ * 128:(cc_ + 1) * 128, :],
                        [g + "_t"], [f"{key}{cc_}"], f"s2ld_{g}{cc_}_{lh}")
            sch.checkpoint("pool")
        xb1 = TPB + 3 + lh * 512
        for mc in range(8):
            bkc = 0 + 2 * (mc % 2)
            bki = 1 + 2 * (mc % 2)
            for bk, wname, ut, ukey in ((bkc, "wc", s_ug, "s_ug"),
                                        (bki, "wi", s_ug2, "s_ug2")):
                for dc in range(8):
                    sch.add("pe", lambda eng, p=pb[bk], w=wname, d_=dc, m=mc,
                            x0=xb1:
                            eng.matmul(p[:], s_wg[w][d_][:, m * 128:(m + 1) * 128],
                                       xnfm[d_][:, x0:x0 + 512],
                                       start=(d_ == 0), stop=False),
                            reads=[f"s_{wname}{dc}", XN[dc]], writes=[PB[bk]],
                            name=f"s2x{wname}{lh}_{mc}_{dc}")
                for cc_ in range(8):
                    sch.add("pe", lambda eng, p=pb[bk], t=ut, c_=cc_, m=mc:
                            eng.matmul(p[:], t[c_][:, m * 128:(m + 1) * 128],
                                       s_c1b[c_][:], start=False, stop=(c_ == 7)),
                            reads=[f"{ukey}{cc_}", f"s_c1b{cc_}"], writes=[PB[bk]],
                            name=f"s2c{wname}{lh}_{mc}_{cc_}")
            sch.add("act", lambda eng, p=pb[bkc], m=mc: eng.activation(
                out=s_gt[0][:], in_=p[:], func=AF.Tanh,
                bias=s_gbias[:, 16 + m:16 + m + 1]),
                reads=[PB[bkc]], scalar_reads=["s_gbias"], writes=["s_gt0"],
                name=f"s2ec{lh}_{mc}")
            sch.add("act", lambda eng, p=pb[bki], m=mc: eng.activation(
                out=s_gt[1][:], in_=p[:], func=AF.Sigmoid,
                bias=s_gbias[:, 8 + m:8 + m + 1]),
                reads=[PB[bki]], scalar_reads=["s_gbias"], writes=["s_gt1"],
                name=f"s2ei{lh}_{mc}")
            sch.add("dve", lambda eng, m=mc: eng.tensor_mul(
                s_c2a[m][:], s_gt[0][:], s_gt[1][:]),
                reads=["s_gt0", "s_gt1"], writes=[f"s_c2a{mc}"],
                name=f"s2m{lh}_{mc}")
        for cc_ in range(8):
            sch.dma("pool", s_ug[cc_][:], wffn["uf"][cc_ * 128:(cc_ + 1) * 128, :],
                    ["uf_t"], [f"s_ug{cc_}"], f"s2ld_uf{cc_}_{lh}")
        sch.checkpoint("pool")
        for mc in range(8):
            bk = 0 + mc % 2
            for dc in range(8):
                sch.add("pe", lambda eng, p=pb[bk], d_=dc, m=mc, x0=xb1:
                        eng.matmul(p[:], s_wg["wf"][d_][:, m * 128:(m + 1) * 128],
                                   xnfm[d_][:, x0:x0 + 512],
                                   start=(d_ == 0), stop=False),
                        reads=[f"s_wf{dc}", XN[dc]], writes=[PB[bk]],
                        name=f"s2xf{lh}_{mc}_{dc}")
            for cc_ in range(8):
                sch.add("pe", lambda eng, p=pb[bk], c_=cc_, m=mc:
                        eng.matmul(p[:], s_ug[c_][:, m * 128:(m + 1) * 128],
                                   s_c1b[c_][:], start=False, stop=(c_ == 7)),
                        reads=[f"s_ug{cc_}", f"s_c1b{cc_}"], writes=[PB[bk]],
                        name=f"s2cf{lh}_{mc}_{cc_}")
            sch.add("act", lambda eng, p=pb[bk], m=mc: eng.activation(
                out=s_gt[0][:], in_=p[:], func=AF.Sigmoid,
                bias=s_gbias[:, 0 + m:0 + m + 1]),
                reads=[PB[bk]], scalar_reads=["s_gbias"], writes=["s_gt0"],
                name=f"s2ef{lh}_{mc}")
            sch.add("dve", lambda eng, m=mc: eng.tensor_mul(
                s_gt[1][:], s_gt[0][:], s_c1[m][:]),
                reads=["s_gt0", f"s_c1_{mc}"], writes=["s_gt1"],
                name=f"s2fm{lh}_{mc}")
            sch.add("dve", lambda eng, m=mc: eng.tensor_add(
                s_c2a[m][:], s_c2a[m][:], s_gt[1][:]),
                reads=["s_gt1", f"s_c2a{mc}"], writes=[f"s_c2a{mc}"],
                name=f"s2fa{lh}_{mc}")
            sch.add("pool", lambda eng, m=mc: eng.tensor_copy(
                s_c2st[:], s_c2a[m][:]),
                reads=[f"s_c2a{mc}"], writes=["s_c2st"], name=f"s2b{lh}_{mc}")
            sch.dma("sp", c2_d[mc * 128:(mc + 1) * 128, lh * 512:(lh + 1) * 512],
                    s_c2st[:], ["s_c2st"], ["c2_d"], f"st_c2{lh}_{mc}")
    AR.pop()

    AR.push()
    s_wo = [AR.alloc(f"s_wo{d_}", [128, D], BF16) for d_ in range(8)]
    s_uo = [AR.alloc(f"s_uo{d_}", [128, D], BF16) for d_ in range(8)]
    s_c2b = [AR.alloc(f"s_c2bb{m}", [128, LLOC], BF16) for m in range(8)]
    for mc in range(8):
        sch.dma("sp", s_c2b[mc][:], c2_d[mc * 128:(mc + 1) * 128, :],
                ["c2_d"], [f"s_c2b{mc}"], f"ld_c2{mc}")
    for dc in range(8):
        sch.dma("pool", s_wo[dc][:], wffn["wo"][dc * 128:(dc + 1) * 128, :],
                ["wo_t"], [f"s_wo{dc}"], f"ld_wo{dc}")
        sch.dma("pool", s_uo[dc][:], wffn["uo"][dc * 128:(dc + 1) * 128, :],
                ["uo_t"], [f"s_uo{dc}"], f"ld_uo{dc}")
    sch.checkpoint("pool")
    s_U = [AR.alloc(f"s_U{l}", [128, D], F32) for l in range(8)]
    s_ht = [AR.alloc(f"s_ht{i}", [128, D], F32) for i in range(2)]
    s_ht2 = [AR.alloc(f"s_ht2{i}", [128, D], F32) for i in range(2)]
    for lc in range(8):
        bks = (0, 1) if lc % 2 == 0 else (2, 3)
        for q in range(2):
            for cc_ in range(8):
                sch.add("pe", lambda eng, p=pb[bks[q]], c_=cc_, l=lc, q_=q:
                        eng.matmul(p[:], s_c2b[c_][:, l * 128:(l + 1) * 128],
                                   s_uo[c_][:, q_ * 512:(q_ + 1) * 512],
                                   start=(c_ == 0), stop=(c_ == 7)),
                        reads=[f"s_c2b{cc_}", f"s_uo{cc_}"], writes=[PB[bks[q]]],
                        name=f"U{lc}_{q}_{cc_}")
            sch.add("dve", lambda eng, p=pb[bks[q]], l=lc, q_=q: eng.tensor_add(
                s_U[l][:, q_ * 512:(q_ + 1) * 512], p[:],
                s_bo_bc[:, q_ * 512:(q_ + 1) * 512]),
                reads=[PB[bks[q]], "s_bo_bc"], writes=[f"s_U{lc}"],
                name=f"Ue{lc}_{q}")
    for bsel in range(B):
        for lc in range(8):
            i = lc % 2
            bks = (4, 5) if i == 0 else (6, 7)
            tcn = bsel * 8 + lc
            for q in range(2):
                for dc in range(8):
                    sch.add("pe", lambda eng, p=pb[bks[q]], d_=dc, t=tcn, q_=q:
                            eng.matmul(p[:], xnfm[d_][:, tcol(t):tcol(t) + 128],
                                       s_wo[d_][:, q_ * 512:(q_ + 1) * 512],
                                       start=(d_ == 0), stop=(d_ == 7)),
                            reads=[XN[dc], f"s_wo{dc}"], writes=[PB[bks[q]]],
                            name=f"h{tcn}_{q}_{dc}")
                sch.add("dve", lambda eng, p=pb[bks[q]], l=lc, j=i, q_=q:
                        eng.tensor_add(s_ht[j][:, q_ * 512:(q_ + 1) * 512],
                                       p[:], s_U[l][:, q_ * 512:(q_ + 1) * 512]),
                        reads=[PB[bks[q]], f"s_U{lc}"], writes=[f"s_ht{i}"],
                        name=f"he{tcn}_{q}")
            sch.add("act", lambda eng, j=i: eng.activation(
                out=s_ht2[j][:], in_=s_ht[j][:], func=AF.Tanh),
                reads=[f"s_ht{i}"], writes=[f"s_ht2{i}"], name=f"ht{tcn}")
            sch.dma("act", h_t[tcn * 128:(tcn + 1) * 128, :], s_ht2[i][:],
                    [f"s_ht2{i}"], ["h_t"], f"st_h{tcn}")
    AR.pop()
    AR.pop()
    sch.barrier()

    env = dict(locals())
    if stage < 5:
        return _finish(nc, sch, ctx, AR, dbg, env)

    AR.push()
    s_nwrow = AR.alloc("s_nwrow", [1, DIN], F32)
    s_nwbc = AR.alloc("s_nwbc", [128, DIN], F32)
    s_cin = [AR.alloc(f"s_cin{b_}", [128, 2048], F32) for b_ in range(B)]
    s_stile = [AR.alloc(f"s_stile{i}", [128, 2048], F32) for i in range(2)]
    s_t2 = AR.alloc("s_t2", [128, 2048], F32)
    s_dgm = AR.alloc("s_dgm", [9, 64], F32)
    s_wsm = AR.alloc("s_wsm", [8, 64], F32)
    s_wline = AR.alloc("s_wline", [1, 512], F32)
    s_wbc = AR.alloc("s_wbc", [128, NH], F32)
    s_hcds = AR.alloc("s_hcds", [128, 8 * NH], F32)
    s_cing = [AR.alloc(f"s_cing{i}", [128, 2048], BF16) for i in range(2)]
    s_xtokB = [AR.alloc(f"s_xtokB{i}", [128, 2176], BF16) for i in range(2)]
    s_sdtxB = [AR.alloc(f"s_sdtxB{i}", [128, ZN], BF16) for i in range(2)]
    s_sc1B = AR.alloc("s_sc1B", [128, 64], F32)
    s_dx = [AR.alloc(f"s_dx{i}", [128, 2048], F32) for i in range(2)]
    s_zrd = [AR.alloc(f"s_zrd{i}", [128, ZN], BF16) for i in range(2)]
    s_prd = [AR.alloc(f"s_prd{i}", [128, 2048], BF16) for i in range(2)]
    s_cbt0 = AR.alloc("s_cbt0", [128, 256], BF16)
    s_cbt1 = AR.alloc("s_cbt1", [128, 128], BF16)
    s_y = [AR.alloc(f"s_y{i}", [128, 2048], F32) for i in range(2)]
    s_gbf = [AR.alloc(f"s_gbf{i}", [128, 2048], BF16) for i in range(2)]
    s_rcol = AR.alloc("s_rcol", [128, 32], F32)

    sch.dma("sp", s_nwrow[:], norm_w_row, [], ["s_nwrow"], "ld_nw",
            full128=False)
    sch.checkpoint("sp")
    for q in range(4):
        bcast_row(s_nwrow[0:1, q * 512:(q + 1) * 512], "s_nwrow",
                  s_nwbc[:, q * 512:(q + 1) * 512], "s_nwbc", 512, q, f"nw{q}")

    for m in range(8):
        for bsel in range(B):
            sch.dma("sp", s_dgm[m + 1:m + 2, bsel * NH:(bsel + 1) * NH],
                    cc_out[m * CCROWS + B * 128 + bsel:
                           m * CCROWS + B * 128 + bsel + 1, 0:NH],
                    ["cc_out"], ["s_dgm"], f"ld_dg{m}_{bsel}", full128=False)
    sch.checkpoint("sp")
    sch.add("pool", lambda eng: eng.memset(s_dgm[0:1, :], 1.0),
            reads=[], writes=["s_dgm"], name="dgm_ones")
    sch.add("pe", lambda eng: eng.matmul(pb[4][0:8, 0:64], s_m2t[:], s_dgm[:],
                                         start=True, stop=True),
            reads=["s_m2t", "s_dgm"], writes=[PB[4]], name="wmm")
    sch.add("act", lambda eng: eng.activation(out=s_wsm[:], in_=pb[4][0:8, 0:64],
                                              func=AF.Exp),
            reads=[PB[4]], writes=["s_wsm"], name="wexp")
    sch.dma("sp", w_d, s_wsm[:], ["s_wsm"], ["w_d"], "st_wd", full128=False)
    sch.checkpoint("sp")
    for j in range(7):
        for bsel in range(B):
            jb = j * 2 + bsel
            sch.dma("sp", s_wline[0:1, jb * NH:(jb + 1) * NH],
                    w_d[j:j + 1, bsel * NH:(bsel + 1) * NH], ["w_d"],
                    ["s_wline"], f"ld_wl{jb}", full128=False)
    sch.checkpoint("sp")
    for bsel in range(B):
        sch.add("pool", lambda eng, b_=bsel: eng.memset(s_cin[b_][:], 0.0),
                reads=[], writes=[f"s_cin{bsel}"], name=f"cin0_{bsel}")
    for j in range(7):
        for bsel in range(B):
            jb = j * 2 + bsel
            i = jb % 2
            sch.dma("act", s_stile[i][:],
                    cc_out[j * CCROWS + bsel * 128:j * CCROWS + (bsel + 1) * 128, :],
                    ["cc_out"], [f"s_stile{i}"], f"ld_S{jb}")
            sch.add("pe", lambda eng, c_=jb: eng.matmul(
                pb[5][0:128, 0:NH], s_ones_row[:],
                s_wline[0:1, c_ * NH:(c_ + 1) * NH], start=True, stop=True),
                reads=["s_ones_row", "s_wline"], writes=[PB[5]], name=f"wbc{jb}")
            sch.add("dve", lambda eng: eng.tensor_copy(s_wbc[:], pb[5][0:128, 0:NH]),
                    reads=[PB[5]], writes=["s_wbc"], name=f"wbce{jb}")
            sch.add("pool", lambda eng, j_=i: eng.tensor_tensor(
                s_t2[:], s_stile[j_][:], rep_inner(s_wbc, NH, HD), ALU.mult),
                reads=[f"s_stile{i}", "s_wbc"], writes=["s_t2"], name=f"cmb{jb}")
            sch.add("pool", lambda eng, b_=bsel: eng.tensor_add(
                s_cin[b_][:], s_cin[b_][:], s_t2[:]),
                reads=["s_t2", f"s_cin{bsel}"], writes=[f"s_cin{bsel}"],
                name=f"cma{jb}")
    if debug:
        for b_ in range(B):
            sch.dma("sp", dbg["dbg_cin"][b_ * 128:(b_ + 1) * 128, :],
                    s_cin[b_][:], [f"s_cin{b_}"], ["dbg_cin"], f"dmp_cin{b_}")
    for bsel in range(B):
        k0 = bsel * 4
        sch.add("dve", lambda eng, k=k0: eng.tensor_copy(
            s_hcds[:, k * NH:(k + 1) * NH], s_cds[:, k * NH:(k + 1) * NH]),
            reads=["s_cds"], writes=["s_hcds"], name=f"hc0_{bsel}")
        for c in range(1, 4):
            gap("dve")
            gap("dve")
            sch.add("dve", lambda eng, k=k0 + c: eng.tensor_mul(
                s_hcds[:, k * NH:(k + 1) * NH],
                s_hcds[:, (k - 1) * NH:k * NH], s_cds[:, k * NH:(k + 1) * NH]),
                reads=["s_cds", "s_hcds"], writes=["s_hcds"], name=f"hc{bsel}_{c}")
    if debug:
        sch.add("pool", lambda eng: eng.tensor_copy(s_wbc[:], s_wbc[:]),
                reads=["s_hcds"], writes=[], name="hcfence")
        sch.dma("sp", dbg["dbg_small"][0:128, 832:832 + 256].rearrange(
            "p c -> p c") if False else dbg["dbg_small"][0:128, 768:1024],
            s_hcds[:], ["s_hcds"], ["dbg_small"], "dmp_hc")

    for bsel in range(B):
        for c in range(NCH):
            bc = bsel * NCH + c
            tcs = (2 * bc, 2 * bc + 1)
            ic_rot = bc % 2
            for j, tc in enumerate(tcs):
                inap = bass.AP(xbc_f.tensor, tc * 128, [[TMAIN, 2176], [1, 128]])
                sch.dma("sp", s_xtokB[j][:], inap, ["xbc_f"], [f"s_xtokB{j}"],
                        f"tpxB{tc}", transpose=True)
                sch.dma("act", s_zrd[j][:], z_t[tc * 128:(tc + 1) * 128, :],
                        ["z_t"], [f"s_zrd{j}"], f"ldz{tc}")
            if c > 0:
                sch.dma("act", s_prd[ic_rot][:],
                        prevsc_d[(bsel * 3 + c - 1) * 128:(bsel * 3 + c) * 128, :],
                        ["prevsc_d"], [f"s_prd{ic_rot}"], f"ldpv{bc}")
            sch.checkpoint("sp")
            sch.checkpoint("act")
            for j, tc in enumerate(tcs):
                sch.add("dve", lambda eng, t=tc, j_=j: eng.tensor_mul(
                    s_sc1B[:, j_ * NH:(j_ + 1) * NH],
                    s_dtsp[:, t * NH:(t + 1) * NH],
                    s_decay[:, t * NH:(t + 1) * NH]),
                    reads=["s_dtsp", "s_decay"], writes=["s_sc1B"], name=f"scB{tc}")
                sch.add("dve", lambda eng, j_=j: eng.tensor_tensor(
                    s_sdtxB[j_][:], s_xtokB[j_][:, 0:2048],
                    rep_inner(s_sc1B, NH, HD, col0=j_ * NH), ALU.mult),
                    reads=[f"s_xtokB{j}", "s_sc1B"], writes=[f"s_sdtxB{j}"],
                    name=f"sdtxB{tc}")
                sch.add("pool", lambda eng, j_=j: eng.tensor_tensor(
                    s_dx[j_][:], s_xtokB[j_][:, 0:2048],
                    rep_inner(s_d_bc, NH, HD), ALU.mult),
                    reads=[f"s_xtokB{j}", "s_d_bc"], writes=[f"s_dx{j}"],
                    name=f"dxB{tc}")
            sch.add("pool", lambda eng, b_=bsel, k=bc, j_=ic_rot:
                    eng.tensor_tensor(s_cing[j_][:], s_cin[b_][:],
                                      rep_inner(s_hcds, NH, HD, col0=k * NH),
                                      ALU.mult),
                    reads=[f"s_cin{bsel}", "s_hcds"], writes=[f"s_cing{ic_rot}"],
                    name=f"cing{bc}")
            tb0 = bsel * 1024 + c * 256
            sch.add("pe", lambda eng, t0=tb0: eng.matmul(
                pb[0][0:128, 0:256], s_Bfm[:, t0:t0 + 128],
                s_Cfm[:, t0:t0 + 256], start=True, stop=True),
                reads=["s_Bfm", "s_Cfm"], writes=[PB[0]], name=f"cbt0_{bc}")
            sch.add("pe", lambda eng, t0=tb0: eng.matmul(
                pb[1][0:128, 0:128], s_Bfm[:, t0 + 128:t0 + 256],
                s_Cfm[:, t0 + 128:t0 + 256], start=True, stop=True),
                reads=["s_Bfm", "s_Cfm"], writes=[PB[1]], name=f"cbt1_{bc}")
            sch.add("dve", lambda eng: eng.tensor_tensor(
                s_cbt0[:], pb[0][0:128, 0:256], s_mask256[:], ALU.mult),
                reads=[PB[0], "s_mask256"], writes=["s_cbt0"], name=f"cbtm0_{bc}")
            sch.add("dve", lambda eng: eng.tensor_tensor(
                s_cbt1[:], pb[1][0:128, 0:128], s_tri[:], ALU.mult),
                reads=[PB[1], "s_tri"], writes=["s_cbt1"], name=f"cbtm1_{bc}")
            for ic in range(2):
                tc = tcs[ic]
                qb = 0 if ic == 0 else 4
                for q in range(4):
                    bk = qb + q
                    srcs = []
                    srcs.append((s_cbt0[:, ic * 128:(ic + 1) * 128],
                                 "s_cbt0", s_sdtxB[0], "s_sdtxB0"))
                    if ic == 1:
                        srcs.append((s_cbt1[:, 0:128], "s_cbt1",
                                     s_sdtxB[1], "s_sdtxB1"))
                    cslice = s_Cfm[:, tb0 + ic * 128:tb0 + (ic + 1) * 128]
                    if c > 0:
                        srcs.append((cslice, "s_Cfm", s_prd[ic_rot],
                                     f"s_prd{ic_rot}"))
                    srcs.append((cslice, "s_Cfm", s_cing[ic_rot],
                                 f"s_cing{ic_rot}"))
                    n = len(srcs)
                    for si, (lh_, lk, rt, rk) in enumerate(srcs):
                        sch.add("pe", lambda eng, p=pb[bk], l=lh_, r=rt, q_=q,
                                st=(si == 0), sp_=(si == n - 1): eng.matmul(
                                    p[:], l, r[:, q_ * 512:(q_ + 1) * 512],
                                    start=st, stop=sp_),
                                reads=[lk, rk], writes=[PB[bk]],
                                name=f"Y{bc}_{ic}_{q}_{si}")
                yt, ytk = s_y[ic], f"s_y{ic}"
                for q in range(4):
                    sch.add("dve", lambda eng, p=pb[qb + q], t=tc, q_=q, y=yt:
                            eng.tensor_tensor(
                                y[:, q_ * 512:(q_ + 1) * 512], p[:],
                                rep_inner(s_rowsc, 8, HD, col0=t * NH + q_ * 8),
                                ALU.mult),
                            reads=[PB[qb + q], "s_rowsc"], writes=[ytk],
                            name=f"ye{bc}_{ic}_{q}")
                sch.add("pool", lambda eng, j_=ic, y=yt: eng.tensor_add(
                    y[:], y[:], s_dx[j_][:]),
                    reads=[ytk, f"s_dx{ic}"], writes=[ytk], name=f"yd{bc}_{ic}")
                sch.add("dve", lambda eng, j_=ic, y=yt: eng.tensor_mul(
                    y[:], y[:], s_zrd[j_][:]),
                    reads=[ytk, f"s_zrd{ic}"], writes=[ytk], name=f"yz{bc}_{ic}")
                co = 16 * ic
                sch.add("act", lambda eng, j_=ic, y=yt: eng.activation(
                    out=s_dx[j_][:], in_=y[:], func=AF.Square),
                    reads=[ytk], writes=[f"s_dx{ic}"], name=f"ysq{bc}_{ic}")
                sch.add("dve", lambda eng, j_=ic, c_=co: eng.tensor_reduce(
                    s_rcol[:, c_:c_ + 1], s_dx[j_][:], AX.X, ALU.add),
                    reads=[f"s_dx{ic}"], writes=[f"s_rcolq{ic}"],
                    name=f"yrd{bc}_{ic}")
                sch.add("act", lambda eng, c_=co: eng.activation(
                    out=s_rcol[:, c_ + 4:c_ + 5], in_=s_rcol[:, c_:c_ + 1],
                    func=AF.Ln, scale=1.0 / DIN, bias=1e-5),
                    reads=[f"s_rcolq{ic}", "s_constrow"],
                    writes=[f"s_rcoll{ic}"], name=f"yln{bc}_{ic}")
                gap("act")
                sch.add("act", lambda eng, c_=co: eng.activation(
                    out=s_rcol[:, c_ + 8:c_ + 9], in_=s_rcol[:, c_ + 4:c_ + 5],
                    func=AF.Exp, scale=-0.5),
                    reads=[f"s_rcoll{ic}"], writes=[f"s_rcolr{ic}"],
                    name=f"yrs{bc}_{ic}")
                sch.add("dve", lambda eng, j_=ic, c_=co, y=yt: eng.scalar_tensor_tensor(
                    s_gbf[j_][:], y[:], s_rcol[:, c_ + 8:c_ + 9],
                    s_nwbc[:], ALU.mult, ALU.mult),
                    reads=[ytk, "s_nwbc"], scalar_reads=[f"s_rcolr{ic}"],
                    writes=[f"s_gbf{ic}"], name=f"ygn{bc}_{ic}")
                sch.dma("sp", g_t[tc * 128:(tc + 1) * 128, :], s_gbf[ic][:],
                        [f"s_gbf{ic}"], ["g_t"], f"st_g{tc}")
    AR.pop()
    sch.barrier()

    env = dict(locals())
    if stage < 6:
        return _finish(nc, sch, ctx, AR, dbg, env)

    AR.push()
    s_gfm = [AR.alloc(f"s_gfm{d_}", [128, TMAIN], BF16) for d_ in range(16)]
    s_wou = [AR.alloc(f"s_wou{d_}", [128, D], BF16) for d_ in range(16)]
    s_ur = [AR.alloc(f"s_urr{i}", [128, D], F32) for i in range(2)]
    s_hr = [AR.alloc(f"s_hrr{i}", [128, D], F32) for i in range(2)]
    s_o = [AR.alloc(f"s_o{i}", [128, D], F32) for i in range(2)]
    for dc in range(16):
        sch.dma("pool", s_wou[dc][:], w_out_t[dc * 128:(dc + 1) * 128, :],
                ["w_out_t"], [f"s_wou{dc}"], f"ld_wout{dc}")
    sch.checkpoint("pool")
    for dc in range(16):
        inap = bass.AP(g_t.tensor, dc * 128, [[DIN, TMAIN], [1, 128]])
        sch.dma("sp", s_gfm[dc][:], inap, ["g_t"], [f"s_gfm{dc}"],
                f"tp_g{dc}", transpose=True)
    sch.checkpoint("sp")
    for tc in range(NTC):
        bsel, r0, i = tc // 8, (tc % 8) * 128, tc % 2
        bks = (0, 1) if i == 0 else (2, 3)
        for q in range(2):
            for dc in range(16):
                sch.add("pe", lambda eng, p=pb[bks[q]], d_=dc, t=tc, q_=q:
                        eng.matmul(p[:], s_gfm[d_][:, t * 128:(t + 1) * 128],
                                   s_wou[d_][:, q_ * 512:(q_ + 1) * 512],
                                   start=(d_ == 0), stop=(d_ == 15)),
                        reads=[f"s_gfm{dc}", f"s_wou{dc}"], writes=[PB[bks[q]]],
                        name=f"op{tc}_{q}_{dc}")
        sch.dma("sp", s_ur[i][:], u[bsel, r0:r0 + 128, :], ["u"],
                [f"s_urr{i}"], f"ldu2_{tc}")
        sch.dma("act", s_hr[i][:], h_t[tc * 128:(tc + 1) * 128, :], ["h_t"],
                [f"s_hrr{i}"], f"ldh{tc}")
        for q in range(2):
            sch.add("dve", lambda eng, p=pb[bks[q]], j=i, q_=q: eng.tensor_add(
                s_o[j][:, q_ * 512:(q_ + 1) * 512],
                s_ur[j][:, q_ * 512:(q_ + 1) * 512], p[:]),
                reads=[PB[bks[q]], f"s_urr{i}"], writes=[f"s_o{i}"],
                name=f"oa{tc}_{q}")
        sch.add("pool", lambda eng, j=i: eng.tensor_add(
            s_o[j][:], s_o[j][:], s_hr[j][:]),
            reads=[f"s_hrr{i}", f"s_o{i}"], writes=[f"s_o{i}"], name=f"oh{tc}")
        sch.dma("act", out[bsel, r0:r0 + 128, :], s_o[i][:], [f"s_o{i}"],
                ["out"], f"st_o{tc}")
    AR.pop()

    env = dict(locals())
    return _finish(nc, sch, ctx, AR, dbg, env)


def _finish(nc, sch, ctx, AR, dbg, env):
    pairs = [("dbg_c2", "c2_d"),
             ("dbg_xn", "xn_t"), ("dbg_xbc", "xbc_f"), ("dbg_z", "z_t"),
             ("dbg_g", "g_t"), ("dbg_h", "h_t"), ("dbg_prevsc", "prevsc_d"),
             ("dbg_ccin", "cc_in"), ("dbg_ccout", "cc_out")]
    for dn, srcn in pairs:
        if dn in dbg and srcn in env:
            src = env[srcn]
            srcap = src if isinstance(src, bass.AP) else src.ap()
            sch.add("act", lambda eng, o=dbg[dn], i=srcap:
                    eng.dma_start(out=o, in_=i),
                    reads=[srcn], writes=[dn], kind="d",
                    name=f"dump_{dn}", full128=False)
    if "dbg_bias" in dbg and "s_biascol" in env:
        t = env["s_biascol"]
        sch.add("act", lambda eng, o=dbg["dbg_bias"], i=t[:]:
                eng.dma_start(out=o, in_=i),
                reads=["s_biascol"], writes=["dbg_bias"], kind="d",
                name="dmp_bias")
    if "dbg_small" in dbg:
        for col0, sbn, rows, cols in [(0, "s_cds", 128, 256),
                                      (256, "s_dalast", 128, 256),
                                      (512, "s_dsumc", 1, 256),
                                      (768, "s_dsum", 1, 64)]:
            if sbn in env:
                t = env[sbn]
                sch.add("act", lambda eng, o=dbg["dbg_small"][0:rows, col0:col0 + cols],
                        i=t[0:rows, 0:cols]: eng.dma_start(out=o, in_=i),
                        reads=[sbn], writes=["dbg_small"], kind="d",
                        name=f"dmpL_{sbn}", full128=False)
    for dn, sbn in [("dbg_dt", "s_dtsp"), ("dbg_dacs", "s_dacs"),
                    ("dbg_decay", "s_decay")]:
        if dn in dbg and sbn in env:
            t = env[sbn]
            sch.add("act", lambda eng, o=dbg[dn], i=t[:]:
                    eng.dma_start(out=o, in_=i),
                    reads=[sbn], writes=[dn], kind="d", name=f"dump_{dn}")
    sch.finalize_and_emit(ctx)
    ctx.close()
    return nc



def make_inputs(full):
    uF = np.ascontiguousarray(full["L0_attn_u"], dtype=np.float32)
    smalls = dict(
        w_in_t=np.ascontiguousarray(full["L0_attn_W_in"].T, dtype=np.float32),
        conv_w_fm=np.ascontiguousarray(
            np.asarray(full["L0_attn_conv_w"], np.float32)
            .reshape(18, 128, 4).transpose(1, 0, 2).reshape(128, 72)),
        conv_b_fm=np.ascontiguousarray(
            np.asarray(full["L0_attn_conv_b"], np.float32).reshape(18, 128).T),
        dt_bias_row=np.asarray(full["L0_attn_dt_bias"], np.float32).reshape(1, NH),
        a_log_row=np.asarray(full["L0_attn_A_log"], np.float32).reshape(1, NH),
        d_param_row=np.asarray(full["L0_attn_D_param"], np.float32).reshape(1, NH),
        norm_w_row=np.asarray(full["L0_attn_norm_w"], np.float32).reshape(1, DIN),
        in_c_col=np.ascontiguousarray(
            np.asarray(full["L0_ffn_in_c"], np.float32).reshape(8, 128).T),
        gbias_col=np.ascontiguousarray(np.concatenate(
            [np.asarray(full["L0_ffn_" + k], np.float32).reshape(8, 128).T
             for k in ("bf", "bi", "bc")], axis=1)),
        bo_row=np.asarray(full["L0_ffn_bo"], np.float32).reshape(1, D),
        ones_row=np.ones((1, 128), np.float32),
        consts_row=np.tile(np.array([[1e-5, 1.0, 0.0, 0.0]], np.float32),
                           (128, 1)),
        tri_f32=np.ascontiguousarray(np.triu(np.ones((128, 128), np.float32))),
        ones_sq=np.ones((128, 128), np.float32),
        mask256=np.ascontiguousarray(np.concatenate(
            [np.triu(np.ones((128, 128), np.float32)),
             np.ones((128, 128), np.float32)], axis=1)),
        w_out_t=np.ascontiguousarray(full["L0_attn_W_out"].T, dtype=np.float32),
    )
    for nm in ("wf", "uf", "wi", "ui", "wc", "uc", "wo", "uo"):
        smalls[nm + "_t"] = np.ascontiguousarray(full["L0_ffn_" + nm].T,
                                                 dtype=np.float32)

    in_maps = []
    for k in range(NCORES):
        l0 = k * LLOC
        m = dict(smalls)
        m["u"] = np.ascontiguousarray(uF[:, l0:l0 + LLOC, :])
        if k == 0:
            m["uh"] = np.zeros((B, 3, D), np.float32)
        else:
            m["uh"] = np.ascontiguousarray(uF[:, l0 - 3:l0, :])
        m2 = np.zeros((9, 8), np.float32)
        for j in range(8):
            for mm in range(8):
                if j < mm < k:
                    m2[mm + 1, j] = 1.0
            if not (j < k):
                m2[0, j] = -1e30
        m["m2t_ext"] = m2
        in_maps.append(m)
    return in_maps


_NC_CACHE = {}


def get_program(stage=99, debug=False):
    key = (stage, debug)
    if key not in _NC_CACHE:
        _NC_CACHE[key] = build_program(stage=stage, debug=debug)
    return _NC_CACHE[key]


def kernel(**inputs):
    in_maps = make_inputs(inputs)
    nc = get_program()
    res = run_bass_kernel_spmd(nc, in_maps, core_ids=list(range(NCORES)))
    outs = [res.results[k]["out"] for k in range(NCORES)]
    return np.concatenate(outs, axis=1)


if __name__ == "__main__":
    nc = build_program(stage=99, debug=False)
    print("built ok")


# version 08 (final)
# speedup vs baseline: 1.0205x; 1.0205x over previous
import numpy as np
import ml_dtypes
import concourse.bass as bass
import concourse.mybir as mybir
from concourse.bass_utils import run_bass_kernel_spmd

F32 = mybir.dt.float32
BF16 = mybir.dt.bfloat16
U8 = mybir.dt.uint8
AF = mybir.ActivationFunctionType
ALU = mybir.AluOpType
AX = mybir.AxisListType

NCORES = 8
B = 2
D = 1024
LLOC = 1024
TMAIN = B * LLOC
TPB = 1152
E = 4384
ZN = 2048
XBCN = 2304
DIN = 2048
NH = 32
HD = 64
CHUNK = 256
NCH = 4
NTC = 16
CCROWS = B * 128 + 2
BF = ml_dtypes.bfloat16


STREAMS = ("pe", "act", "dve", "pool", "sp")


class Op:
    __slots__ = ("stream", "fn", "kind", "deps", "targeted", "idx", "val",
                 "pre_nops", "waits", "name", "full128")

    def __init__(self, stream, fn, kind, name=""):
        self.stream = stream
        self.fn = fn
        self.kind = kind
        self.deps = []
        self.targeted = False
        self.idx = -1
        self.val = None
        self.pre_nops = 0
        self.waits = []
        self.name = name
        self.full128 = True


class Sched:
    def __init__(self, nc):
        self.nc = nc
        self.ops = {s: [] for s in STREAMS}
        self.writers = {}
        self.readers = {}
        self.barrier_ops = []
        self.dram_keys = set()

    def _edge(self, cons, prod, dram=False):
        if prod is cons:
            return
        if prod.stream == cons.stream:
            if prod.kind == "c" and cons.kind == "c":
                return
            if (prod.kind == "d" and cons.kind == "d"
                    and prod.full128 and cons.full128 and not dram):
                return
        cons.deps.append(prod)

    def add(self, stream, fn, reads=(), writes=(), scalar_reads=(), kind="c",
            name="", full128=True, extra_deps=()):
        op = Op(stream, fn, kind, name)
        op.full128 = full128
        my = len(self.ops[stream])
        for key in list(reads) + list(scalar_reads):
            dk = key in self.dram_keys
            for w in self.writers.get(key, ()):
                self._edge(op, w, dram=dk)
        for key in writes:
            dk = key in self.dram_keys
            for r in self.readers.get(key, ()):
                self._edge(op, r, dram=dk)
            for w in self.writers.get(key, ()):
                self._edge(op, w, dram=dk)
        for d in extra_deps:
            if d is not None:
                self._edge(op, d)
        for bo in self.barrier_ops:
            self._edge(op, bo)
        op.idx = my
        self.ops[stream].append(op)
        for key in list(reads) + list(scalar_reads):
            self.readers.setdefault(key, []).append(op)
        for key in writes:
            self.writers.setdefault(key, []).append(op)
            self.readers[key] = []
        return op

    def dma(self, stream, outap, inap, reads, writes, name="", full128=True,
            transpose=False):
        if transpose:
            fn = lambda eng, o=outap, i=inap: eng.dma_start_transpose(o, i)
        else:
            fn = lambda eng, o=outap, i=inap: eng.dma_start(out=o, in_=i)
        return self.add(stream, fn, reads=reads, writes=writes, kind="d",
                        name=name, full128=full128)

    def barrier(self):
        targets = []
        for s in STREAMS:
            if self.ops[s]:
                targets.append(self.ops[s][-1])
        self.barrier_ops = targets

    def checkpoint(self, stream):
        last_d = None
        for op in reversed(self.ops[stream]):
            if op.kind == "d":
                last_d = op
                break
        cp = self.add(stream, lambda eng: eng.nop(), kind="c",
                      name=f"cp_{stream}",
                      extra_deps=[last_d] if last_d else [])
        for key, ws in list(self.writers.items()):
            if any(w.stream == stream and w.kind == "d" for w in ws):
                keep = [w for w in ws if not (w.stream == stream and w.kind == "d")]
                self.writers[key] = keep + [cp]
        return cp

    def finalize_and_emit(self, ctx):
        nc = self.nc
        sems = {}
        for s in STREAMS:
            sems[(s, "c")] = ctx.enter_context(nc.semaphore(f"sem_{s}_c"))
            sems[(s, "d")] = ctx.enter_context(nc.semaphore(f"sem_{s}_d"))
        sems["cc"] = ctx.enter_context(nc.semaphore("sem_cc"))

        for s in STREAMS:
            for op in self.ops[s]:
                if op.kind == "d":
                    op.targeted = True
                for d in op.deps:
                    d.targeted = True
        finals = []
        for s in STREAMS:
            if self.ops[s]:
                last = self.ops[s][-1]
                last.targeted = True
                finals.append(last)
        fin = Op("sp", lambda eng: eng.nop(), "c", "final")
        fin.deps = list(finals)
        fin.idx = len(self.ops["sp"])
        self.ops["sp"].append(fin)

        for s in STREAMS:
            cc, cd = 0, 0
            for op in self.ops[s]:
                if op.kind == "c":
                    if op.targeted:
                        cc += 1
                        op.val = (sems[(s, "c")], cc)
                elif op.kind == "d":
                    if op.targeted:
                        cd += 1
                        op.val = (sems[(s, "d")], 16 * cd)
                else:
                    op.val = (sems["cc"], 1)
        for s in STREAMS:
            clock = {}
            for op in self.ops[s]:
                need = {}
                for d in op.deps:
                    assert d.val is not None, \
                        f"dep of {op.name} has no val: {d.name}"
                    sem, v = d.val
                    if need.get(sem, -1) < v:
                        need[sem] = v
                for sem, v in sorted(need.items(), key=lambda kv: str(kv[0])):
                    if clock.get(sem, 0) < v:
                        op.waits.append((sem, v))
                        clock[sem] = v

        block = ctx.enter_context(nc.Block())

        def emit(stream):
            def body(eng):
                for op in self.ops[stream]:
                    for sem, v in op.waits:
                        eng.wait_ge(sem, v)
                    for _ in range(op.pre_nops):
                        eng.nop()
                    inst = op.fn(eng) if op.fn is not None else None
                    if op.targeted and op.val is not None:
                        if inst is None:
                            inst = eng.nop()
                        sem, _ = op.val
                        amt = 16 if op.kind == "d" else 1
                        inst.then_inc(sem, amt)
            return body

        block.tensor(emit("pe"))
        block.scalar(emit("act"))
        block.vector(emit("dve"))
        block.gpsimd(emit("pool"))
        block.sync(emit("sp"))

    def stats(self):
        return {s: len(self.ops[s]) for s in STREAMS}


class Arena:
    def __init__(self, nc, ctx, nbytes, sch=None):
        self.sch = sch
        self.nc = nc
        self.nbytes = nbytes
        self.backing = ctx.enter_context(
            nc.sbuf_tensor("arena", [128, nbytes], U8))
        self.base = nc.lookup_mloc(self.backing).addr
        self.off = 0
        self.stack = []
        self.peak = 0

    def alloc(self, name, shape, dtype):
        per_part = 1
        for d_ in shape[1:]:
            per_part *= d_
        per_part *= mybir.dt.size(dtype)
        off = (self.off + 31) & ~31
        self.off = off + per_part
        self.peak = max(self.peak, self.off)
        assert self.off <= self.nbytes, \
            f"arena overflow at {name}: {self.off} > {self.nbytes}"
        h = self.nc.virtual_sbuf_tensor(name, shape, dtype)
        self.nc._tensor(h.name, h.shape, h.dtype, type="SB")
        mloc = self.nc.lookup_mloc(h)
        mloc.allocated = True
        mloc.addr = self.base + off
        return h

    def push(self):
        self.stack.append(self.off)

    def pop(self):
        self.off = self.stack.pop()
        if self.sch is not None:
            self.sch.barrier()


def ap3(t, pdim_rows, dims, offset=0):
    a = t[:]
    return bass.AP(a.tensor, a.offset + offset, [[a.ap[0][0], pdim_rows]] + dims)


def rep_inner(t, h_count, rep, col0=0, rows=128):
    a = t[:]
    return bass.AP(a.tensor, a.offset + col0,
                   [[a.ap[0][0], rows], [1, h_count], [0, rep]])


def rep_outer(t, h_count, rep, col0=0, rows=128):
    a = t[:]
    return bass.AP(a.tensor, a.offset + col0,
                   [[a.ap[0][0], rows], [0, rep], [1, h_count]])



LNCOL_DUMP = False


def build_program(stage=99, debug=False):
    from contextlib import ExitStack
    ctx = ExitStack()
    nc = bass.Bass("TRN2", target_bir_lowering=False)
    sch = Sched(nc)

    def din(name, shape, dtype=F32):
        return nc.dram_tensor(name, shape, dtype, kind="ExternalInput").ap()

    u = din("u", [B, LLOC, D])
    uh = din("uh", [B, 3, D])
    w_in_t = din("w_in_t", [D, E])
    conv_w_fm = din("conv_w_fm", [128, 72])
    conv_b_fm = din("conv_b_fm", [128, 18])
    dt_bias_row = din("dt_bias_row", [1, NH])
    a_log_row = din("a_log_row", [1, NH])
    d_param_row = din("d_param_row", [1, NH])
    norm_w_row = din("norm_w_row", [1, DIN])
    in_c_col = din("in_c_col", [128, 8])
    gbias_col = din("gbias_col", [128, 24])
    bo_row = din("bo_row", [1, D])
    ones_row = din("ones_row", [1, 128])
    consts_row = din("consts_row", [128, 4])
    tri_f32 = din("tri_f32", [128, 128])
    ones_sq = din("ones_sq", [128, 128])
    mask256 = din("mask256", [128, 256])
    m2t_ext = din("m2t_ext", [9, 8])
    w_out_t = din("w_out_t", [DIN, D])
    wffn = {}
    for nm in ("wf", "uf", "wi", "ui", "wc", "uc", "wo", "uo"):
        wffn[nm] = din(nm + "_t", [D, D])

    out = nc.dram_tensor("out", [B, LLOC, D], F32, kind="ExternalOutput").ap()

    xn_t = nc.dram_tensor("xn_t", [B * TPB, D], BF16).ap()
    xbc_f = nc.dram_tensor("xbc_f", [XBCN, TMAIN], BF16).ap()
    z_t = nc.dram_tensor("z_t", [TMAIN, ZN], BF16).ap()
    g_t = nc.dram_tensor("g_t", [TMAIN, DIN], BF16).ap()
    h_t = nc.dram_tensor("h_t", [TMAIN, D], F32).ap()
    prevsc_d = nc.dram_tensor("prevsc_d", [B * 3 * 128, NH * HD], BF16).ap()
    c2_d = nc.dram_tensor("c2_d", [D, LLOC], BF16).ap()
    cc_in_h = nc.dram_tensor("cc_in", [CCROWS, NH * HD], F32)
    cc_out_h = nc.dram_tensor("cc_out", [NCORES * CCROWS, NH * HD], F32)
    cc_in = cc_in_h.ap()
    cc_out = cc_out_h.ap()
    w_d = nc.dram_tensor("w_d", [8, 64], F32).ap()
    sch.dram_keys.update(["u", "uh", "xn_t", "xbc_f", "z_t", "g_t", "h_t",
                          "prevsc_d", "cc_in", "cc_out", "w_d", "c2_d", "out"])

    dbg = {}
    if debug:
        for nm, shape, dt_ in [
            ("dbg_xn", [B * TPB, D], BF16), ("dbg_xbc", [XBCN, TMAIN], BF16),
            ("dbg_z", [TMAIN, ZN], BF16), ("dbg_dt", [128, 512], F32),
            ("dbg_dacs", [128, 512], F32), ("dbg_decay", [128, 512], F32),
            ("dbg_ccin", [CCROWS, NH * HD], F32),
            ("dbg_ccout", [NCORES * CCROWS, NH * HD], F32),
            ("dbg_g", [TMAIN, DIN], BF16), ("dbg_h", [TMAIN, D], F32),
            ("dbg_prevsc", [B * 3 * 128, NH * HD], BF16),
            ("dbg_lncol", [NTC * 128, 32], F32),
            ("dbg_xtok", [2 * 128, 2176], BF16),
            ("dbg_st0", [128, 2048], F32),
            ("dbg_small", [128, 1024], F32),
            ("dbg_c2", [D, LLOC], BF16),
            ("dbg_cin", [2 * 128, NH * HD], F32),
            ("dbg_bias", [128, 24], F32),
            ("dbg_sdtx", [2 * 128, ZN], BF16),
        ]:
            dbg[nm] = nc.dram_tensor(nm, shape, dt_, kind="ExternalOutput").ap()

    pb = [ctx.enter_context(nc.psum_tensor(f"pb{i}", [128, 512], F32))
          for i in range(8)]
    PB = [f"pb{i}" for i in range(8)]

    AR = Arena(nc, ctx, 207 * 1024, sch)

    s_convw = AR.alloc("s_convw", [128, 72], F32)
    s_convb = AR.alloc("s_convb", [128, 18], F32)
    s_dtb_row = AR.alloc("s_dtb_row", [1, NH], F32)
    s_alog_row = AR.alloc("s_alog_row", [1, NH], F32)
    s_a_row = AR.alloc("s_a_row", [1, NH], F32)
    s_dp_row = AR.alloc("s_dp_row", [1, NH], F32)
    s_ones_row = AR.alloc("s_ones_row", [1, 128], F32)
    s_constrow = AR.alloc("s_constrow", [128, 4], F32)
    s_tri = AR.alloc("s_tri", [128, 128], F32)
    s_onessq = AR.alloc("s_onessq", [128, 128], F32)
    s_mask256 = AR.alloc("s_mask256", [128, 256], F32)
    s_incol = AR.alloc("s_incol", [128, 8], F32)
    s_incol_bf = AR.alloc("s_incol_bf", [128, 8], BF16)
    s_gbias = AR.alloc("s_gbias", [128, 24], F32)
    s_m2t = AR.alloc("s_m2t", [9, 8], F32)
    s_dtb_bc = AR.alloc("s_dtb_bc", [128, NH], F32)
    s_a_bc = AR.alloc("s_a_bc", [128, NH], F32)
    s_d_bc = AR.alloc("s_d_bc", [128, NH], F32)
    s_bo_bc = AR.alloc("s_bo_bc", [128, D], F32)
    s_Bfm = AR.alloc("s_Bfm", [128, TMAIN], BF16)
    s_Cfm = AR.alloc("s_Cfm", [128, TMAIN], BF16)
    s_gapA = AR.alloc("s_gapA", [128, 16], F32)
    s_gapB = AR.alloc("s_gapB", [128, 16], F32)

    def gap(stream):
        if stream == "act":
            sch.add("act", lambda eng: eng.copy(out=s_gapA[:], in_=s_gapB[:]),
                    name="gap_a")
        elif stream == "pool":
            sch.add("pool", lambda eng: eng.tensor_copy(s_gapA[:], s_gapB[:]),
                    name="gap_p")
        else:
            sch.add("dve", lambda eng: eng.tensor_copy(s_gapA[:], s_gapB[:]),
                    name="gap_v")

    s_dtsp = AR.alloc("s_dtsp", [128, 512], F32)
    s_da = AR.alloc("s_da", [128, 512], F32)
    s_dacs = AR.alloc("s_dacs", [128, 512], F32)
    s_decay = AR.alloc("s_decay", [128, 512], F32)
    s_rowsc = AR.alloc("s_rowsc", [128, 512], F32)
    s_cds = AR.alloc("s_cds", [128, 8 * NH], F32)
    s_dalast = AR.alloc("s_dalast", [128, 8 * NH], F32)
    s_dsumc = AR.alloc("s_dsumc", [1, 8 * NH], F32)
    s_dsum = AR.alloc("s_dsum", [1, 2 * NH], F32)
    s_biascol = AR.alloc("s_biascol", [128, 24], F32)

    sch.dma("sp", s_convw[:], conv_w_fm, ["conv_w_fm"], ["s_convw"], "ld_convw")
    sch.dma("sp", s_convb[:], conv_b_fm, ["conv_b_fm"], ["s_convb"], "ld_convb")
    sch.dma("sp", s_dtb_row[:], dt_bias_row, [], ["s_dtb_row"], "ld_dtb", full128=False)
    sch.dma("sp", s_alog_row[:], a_log_row, [], ["s_alog_row"], "ld_alog", full128=False)
    sch.dma("sp", s_dp_row[:], d_param_row, [], ["s_dp_row"], "ld_dp", full128=False)
    sch.dma("sp", s_ones_row[:], ones_row, [], ["s_ones_row"], "ld_ones", full128=False)
    sch.dma("sp", s_constrow[:], consts_row, [], ["s_constrow"], "ld_consts", full128=False)
    nc.const_aps.register(F32, 1e-5, s_constrow[:, 0:1])
    nc.const_aps.register(F32, 1.0, s_constrow[:, 1:2])
    nc.const_aps.register(F32, 0.0, s_constrow[:, 2:3])
    sch.dma("sp", s_tri[:], tri_f32, [], ["s_tri"], "ld_tri")
    sch.dma("sp", s_onessq[:], ones_sq, [], ["s_onessq"], "ld_onessq")
    sch.dma("sp", s_mask256[:], mask256, [], ["s_mask256"], "ld_mask")
    sch.dma("sp", s_incol[:], in_c_col, [], ["s_incol"], "ld_incol")
    sch.dma("sp", s_gbias[:], gbias_col, [], ["s_gbias"], "ld_gbias")
    sch.dma("sp", s_m2t[:], m2t_ext, [], ["s_m2t"], "ld_m2t", full128=False)
    AR.push()
    s_bo_row = AR.alloc("s_bo_row", [1, D], F32)
    sch.dma("sp", s_bo_row[:], bo_row, [], ["s_bo_row"], "ld_bo", full128=False)
    sch.checkpoint("sp")

    sch.add("act", lambda eng: eng.activation(out=s_a_row[:], in_=s_alog_row[:],
                                              func=AF.Exp),
            reads=["s_alog_row"], writes=["s_a_row"], name="a_exp")
    sch.add("act", lambda eng: eng.mul(out=s_a_row[:], in_=s_a_row[:], mul=-1.0),
            reads=["s_a_row"], writes=["s_a_row"], name="a_neg")
    sch.add("pool", lambda eng: eng.tensor_copy(s_incol_bf[:], s_incol[:]),
            reads=["s_incol"], writes=["s_incol_bf"], name="incbf")

    def bcast_row(srcap, srckey, dstap, dstkey, ncols, bank, name):
        p = pb[bank]
        sch.add("pe", lambda eng: eng.matmul(p[0:128, 0:ncols], s_ones_row[:],
                                             srcap, start=True, stop=True),
                reads=["s_ones_row", srckey], writes=[PB[bank]], name=f"bc_{name}")
        sch.add("dve", lambda eng: eng.tensor_copy(dstap, p[0:128, 0:ncols]),
                reads=[PB[bank]], writes=[dstkey], name=f"bce_{name}")

    bcast_row(s_dtb_row[0:1, :], "s_dtb_row", s_dtb_bc[:], "s_dtb_bc", NH, 0, "dtb")
    bcast_row(s_a_row[0:1, :], "s_a_row", s_a_bc[:], "s_a_bc", NH, 1, "abc")
    bcast_row(s_dp_row[0:1, :], "s_dp_row", s_d_bc[:], "s_d_bc", NH, 2, "dbc")
    for q in range(2):
        bcast_row(s_bo_row[0:1, q * 512:(q + 1) * 512], "s_bo_row",
                  s_bo_bc[:, q * 512:(q + 1) * 512], "s_bo_bc", 512, 3 + q,
                  f"bo{q}")
    AR.pop()

    env = dict(locals())
    if stage < 1:
        return _finish(nc, sch, ctx, AR, dbg, env)

    AR.push()
    xnfm = [AR.alloc(f"xnfm{d_}", [128, B * TPB], BF16) for d_ in range(8)]
    XN = [f"xnfm{d_}" for d_ in range(8)]

    AR.push()
    s_u = [AR.alloc(f"s_u{i}", [128, D], F32) for i in range(2)]
    s_xnst = [AR.alloc(f"s_xnst{i}", [128, D], BF16) for i in range(2)]
    s_uhalo = AR.alloc("s_uhalo", [6, D], F32)
    s_xnhalo = AR.alloc("s_xnhalo", [6, D], BF16)
    s_lncol = [AR.alloc(f"s_lncol{i}", [128, 32], F32) for i in range(3)]
    s_sqscr = [AR.alloc(f"s_sqscr{i}", [128, D], F32) for i in range(2)]
    s_xct = [AR.alloc(f"s_xct{i}", [128, D], F32) for i in range(2)]

    def ln_tile(utile, ukey, xout, xkey, rows, ci):
        c = s_lncol[ci]
        ck = f"s_lncol{ci}"
        sq = s_sqscr[ci % 2]
        sqk = f"s_sqscr{ci % 2}"
        cw = lambda j: c[0:rows, 4 * j:4 * j + 1]
        sch.add("act", lambda eng: eng.activation(
            out=sq[0:rows, :], in_=utile[0:rows, :], func=AF.Square),
            reads=[ukey], writes=[sqk], name="ln_sq")
        sch.add("dve", lambda eng: eng.tensor_reduce(
            cw(0), utile[0:rows, :], AX.X, ALU.add),
            reads=[ukey], writes=[ck], name="ln_s1")
        sch.add("dve", lambda eng: eng.tensor_reduce(
            cw(1), sq[0:rows, :], AX.X, ALU.add),
            reads=[sqk], writes=[ck + "q"], name="ln_ssq")
        sch.add("act", lambda eng: eng.mul(out=cw(2), in_=cw(0), mul=1.0 / D),
                reads=[ck], writes=[ck + "m"], name="ln_m")
        sch.add("dve", lambda eng: eng.tensor_mul(cw(3), cw(2), cw(2)),
                reads=[ck + "m"], writes=[ck + "m2"], name="ln_m2")
        gap("dve")
        sch.add("dve", lambda eng: eng.scalar_tensor_tensor(
            cw(4), cw(1), 1.0 / D, cw(3), ALU.mult, ALU.subtract),
            reads=[ck + "q", ck + "m2"], writes=[ck + "v"], name="ln_v")
        sch.add("act", lambda eng: eng.activation(
            out=cw(5), in_=cw(4), func=AF.Ln, bias=1e-5),
            reads=[ck + "v", "s_constrow"], writes=[ck + "l"], name="ln_ln")
        gap("act")
        sch.add("act", lambda eng: eng.activation(
            out=cw(6), in_=cw(5), func=AF.Exp, scale=-0.5),
            reads=[ck + "l"], writes=[ck + "r"], name="ln_rstd")
        xt_ = s_xct[ci % 2]
        xtk = f"s_xct{ci % 2}"
        sch.add("dve", lambda eng: eng.tensor_scalar_sub(
            xt_[0:rows, :], utile[0:rows, :], cw(2)),
            reads=[ukey], scalar_reads=[ck + "m"],
            writes=[xtk], name="ln_xc")
        sch.add("dve", lambda eng: eng.tensor_scalar_mul(
            xout[0:rows, :], xt_[0:rows, :], cw(6)),
            reads=[xtk], scalar_reads=[ck + "r"],
            writes=[xkey], name="ln_xn")

    sch.dma("sp", s_uhalo[:], uh.rearrange("b r d -> (b r) d"), ["uh"],
            ["s_uhalo"], "ld_uhalo", full128=False)
    sch.checkpoint("sp")
    ln_tile(s_uhalo, "s_uhalo", s_xnhalo, "s_xnhalo", 6, 2)
    for bb in range(B):
        sch.dma("act", xn_t[bb * TPB:bb * TPB + 3, :],
                s_xnhalo[bb * 3:(bb + 1) * 3, :], ["s_xnhalo"], [f"xn_th{bb}"],
                f"st_xnh{bb}", full128=False)
    sch.checkpoint("act")

    s_lnstg = [AR.alloc(f"s_lnstg{i}", [128, 32], F32) for i in range(2)]
    for tc in range(NTC):
        bsel, r0, i = tc // 8, (tc % 8) * 128, tc % 2
        ut, xt = s_u[i], s_xnst[i]
        sch.dma("sp", ut[:], u[bsel, r0:r0 + 128, :], ["u"], [f"s_u{i}"],
                f"ld_u{tc}")
        ln_tile(ut, f"s_u{i}", xt, f"s_xnst{i}", 128, i)
        sch.dma("act", xn_t[bsel * TPB + 3 + r0:bsel * TPB + 3 + r0 + 128, :],
                xt[:], [f"s_xnst{i}"], [f"xn_t{tc}"], f"st_xn{tc}")
        if debug and LNCOL_DUMP:
            ck = f"s_lncol{i}"
            sch.add("pool", lambda eng, j=i: eng.tensor_copy(
                s_lnstg[j][:], s_lncol[j][:]),
                reads=[ck, ck + "q", ck + "m", ck + "m2", ck + "v", ck + "l",
                       ck + "r"],
                writes=[f"s_lnstg{i}"], name=f"lnstg{tc}")
            sch.dma("sp", dbg["dbg_lncol"][tc * 128:(tc + 1) * 128, :],
                    s_lnstg[i][:], [f"s_lnstg{i}"], ["dbg_lncol"],
                    f"st_lncol{tc}")

    for dc in range(8):
        inap = bass.AP(xn_t.tensor, dc * 128, [[D, B * TPB], [1, 128]])
        sch.dma("sp", xnfm[dc][:], inap,
                [f"xn_t{t}" for t in range(NTC)] + ["xn_th0", "xn_th1"],
                [XN[dc]], f"tp_xn{dc}", transpose=True)
    sch.checkpoint("sp")
    AR.pop()

    env = dict(locals())
    if stage < 2:
        return _finish(nc, sch, ctx, AR, dbg, env)

    AR.push()
    win = [AR.alloc(f"win{d_}", [128, E], BF16) for d_ in range(8)]
    WIN = [f"win{d_}" for d_ in range(8)]
    for dc in range(8):
        sch.dma("pool", win[dc][:], w_in_t[dc * 128:(dc + 1) * 128, :],
                ["w_in_t"], [WIN[dc]], f"ld_win{dc}")
    sch.checkpoint("pool")
    s_xpre = [AR.alloc(f"s_xpre{i}", [128, 2054], F32) for i in range(2)]
    s_convt = [AR.alloc(f"s_convt{i}", [128, TMAIN], F32) for i in range(2)]
    s_xbcs = [AR.alloc(f"s_xbcs{i}", [128, TMAIN], BF16) for i in range(2)]

    def tcol(tc):
        return (tc // 8) * TPB + 3 + (tc % 8) * 128

    for ec in range(18):
        i = ec % 2
        xp, xpk = s_xpre[i], f"s_xpre{i}"
        hbank = 2 + i
        for tq in range(4):
            bsel, q = tq // 2, tq % 2
            bank = tq % 2
            for dc in range(8):
                lhsT = win[dc][:, 2048 + ec * 128:2048 + (ec + 1) * 128]
                rhs = xnfm[dc][:, bsel * TPB + 3 + q * 512:
                               bsel * TPB + 3 + (q + 1) * 512]
                sch.add("pe", lambda eng, p=pb[bank], l=lhsT, r=rhs,
                        st=(dc == 0), sp_=(dc == 7): eng.matmul(
                            p[:], l, r, start=st, stop=sp_),
                        reads=[WIN[dc], XN[dc]], writes=[PB[bank]],
                        name=f"ip{ec}_{tq}_{dc}")
                if tq == 0:
                    hap = ap3(xnfm[dc], 128, [[TPB, 2], [1, 3]])
                    sch.add("pe", lambda eng, p=pb[hbank], l=lhsT, r=hap,
                            st=(dc == 0), sp_=(dc == 7): eng.matmul(
                                p[0:128, 0:6], l, r, start=st, stop=sp_),
                            reads=[WIN[dc], XN[dc]], writes=[PB[hbank]],
                            name=f"iph{ec}_{dc}")
            dst = xp[:, bsel * 1027 + 3 + q * 512:bsel * 1027 + 3 + (q + 1) * 512]
            sch.add("act", lambda eng, d=dst, p=pb[bank]: eng.copy(out=d, in_=p[:]),
                    reads=[PB[bank]], writes=[xpk], name=f"ipe{ec}_{tq}")
        hdst = ap3(xp, 128, [[1027, 2], [1, 3]])
        sch.add("act", lambda eng, d=hdst, p=pb[hbank]: eng.copy(
            out=d, in_=p[0:128, 0:6]),
            reads=[PB[hbank]], writes=[xpk], name=f"ipeh{ec}")
        ct, ctk = s_convt[i], f"s_convt{i}"
        eng_sel = "dve"
        for bsel in range(B):
            od = ct[:, bsel * 1024:(bsel + 1) * 1024]
            for k in range(4):
                src = xp[:, bsel * 1027 + k:bsel * 1027 + k + 1024]
                wc = s_convw[:, ec * 4 + k:ec * 4 + k + 1]
                if k == 0:
                    sch.add(eng_sel, lambda eng, o=od, s=src, w=wc:
                            eng.tensor_scalar_mul(o, s, w),
                            reads=[xpk], scalar_reads=["s_convw"], writes=[ctk],
                            name=f"cv{ec}_{bsel}_0")
                else:
                    sch.add(eng_sel, lambda eng, o=od, s=src, w=wc:
                            eng.scalar_tensor_tensor(o, s, w, o, ALU.mult, ALU.add),
                            reads=[xpk, ctk], scalar_reads=["s_convw"],
                            writes=[ctk], name=f"cv{ec}_{bsel}_{k}")
        if ec == 16:
            sil_out, sil_key = s_Bfm, "s_Bfm"
        elif ec == 17:
            sil_out, sil_key = s_Cfm, "s_Cfm"
        else:
            sil_out, sil_key = s_xbcs[i], f"s_xbcs{i}"
        sch.add("act", lambda eng, o=sil_out, s=ct, b_=s_convb[:, ec:ec + 1]:
                eng.activation(out=o[:], in_=s[:], func=AF.Silu, bias=b_),
                reads=[ctk], scalar_reads=["s_convb"], writes=[sil_key],
                name=f"silu{ec}")
        if ec < 17:
            sch.dma("act", xbc_f[ec * 128:(ec + 1) * 128, :], sil_out[:],
                    [sil_key], [f"xbc_f{ec}"], f"st_xbc{ec}")

    s_zst = [AR.alloc(f"s_zst{i}", [128, ZN], BF16) for i in range(2)]
    for tc in range(NTC):
        i = tc % 2
        qb = 0 if i == 0 else 4
        for dc in range(8):
            lhsT = xnfm[dc][:, tcol(tc):tcol(tc) + 128]
            for q in range(4):
                rhs = win[dc][:, q * 512:(q + 1) * 512]
                sch.add("pe", lambda eng, p=pb[qb + q], l=lhsT, r=rhs,
                        st=(dc == 0), sp_=(dc == 7): eng.matmul(
                            p[:], l, r, start=st, stop=sp_),
                        reads=[WIN[dc], XN[dc]], writes=[PB[qb + q]],
                        name=f"z{tc}_{dc}_{q}")
        for q in range(4):
            dst = s_zst[i][:, q * 512:(q + 1) * 512]
            sch.add("act", lambda eng, d=dst, p=pb[qb + q]: eng.activation(
                out=d, in_=p[:], func=AF.Silu),
                reads=[PB[qb + q]], writes=[f"s_zst{i}"], name=f"ze{tc}_{q}")
        sch.dma("act", z_t[tc * 128:(tc + 1) * 128, :], s_zst[i][:],
                [f"s_zst{i}"], [f"z_t{tc}"], f"st_z{tc}")
    for tc in range(NTC):
        bank = tc % 2
        for dc in range(8):
            lhsT = xnfm[dc][:, tcol(tc):tcol(tc) + 128]
            rhs = win[dc][:, 4352:4384]
            sch.add("pe", lambda eng, p=pb[bank], l=lhsT, r=rhs,
                    st=(dc == 0), sp_=(dc == 7): eng.matmul(
                        p[0:128, 0:NH], l, r, start=st, stop=sp_),
                    reads=[WIN[dc], XN[dc]], writes=[PB[bank]],
                    name=f"dt{tc}_{dc}")
        sch.add("dve", lambda eng, t=tc, p=pb[bank]: eng.tensor_copy(
            s_dtsp[:, t * NH:(t + 1) * NH], p[0:128, 0:NH]),
            reads=[PB[bank]], writes=["s_dtsp"], name=f"dte{tc}")
    AR.pop()

    AR.push()
    s_dtA = AR.alloc("s_dtA", [128, 512], F32)
    s_dtB = AR.alloc("s_dtB", [128, 512], F32)
    sch.add("dve", lambda eng: eng.tensor_tensor(
        s_dtsp[:], s_dtsp[:], rep_outer(s_dtb_bc, NH, NTC), ALU.add),
        reads=["s_dtsp", "s_dtb_bc"], writes=["s_dtsp"], name="dt_add")
    sch.add("act", lambda eng: eng.activation(out=s_dtA[:], in_=s_dtsp[:],
                                              func=AF.Abs),
            reads=["s_dtsp"], writes=["s_dtA"], name="dt_abs")
    sch.add("act", lambda eng: eng.activation(out=s_dtA[:], in_=s_dtA[:],
                                              func=AF.Exp, scale=-1.0),
            reads=["s_dtA"], writes=["s_dtA"], name="dt_exp")
    sch.add("act", lambda eng: eng.activation(out=s_dtA[:], in_=s_dtA[:],
                                              func=AF.Ln, bias=1.0),
            reads=["s_dtA", "s_constrow"], writes=["s_dtA"], name="dt_ln")
    sch.add("dve", lambda eng: eng.tensor_scalar_max(s_dtB[:], s_dtsp[:], 0.0),
            reads=["s_dtsp"], writes=["s_dtB"], name="dt_max")
    sch.add("dve", lambda eng: eng.tensor_add(s_dtsp[:], s_dtA[:], s_dtB[:]),
            reads=["s_dtA", "s_dtB"], writes=["s_dtsp"], name="dt_sp")
    sch.add("dve", lambda eng: eng.tensor_tensor(
        s_da[:], s_dtsp[:], rep_outer(s_a_bc, NH, NTC), ALU.mult),
        reads=["s_dtsp", "s_a_bc"], writes=["s_da"], name="dt_da")
    for bc in range(8):
        t0, t1 = 2 * bc, 2 * bc + 1
        b0 = pb[bc % 2]
        b1 = pb[2 + bc % 2]
        br = pb[4 + bc % 2]
        bl = pb[6 + bc % 2]
        daf = lambda t: s_da[:, t * NH:(t + 1) * NH]
        sch.add("pe", lambda eng, p=b0, r=daf(t0): eng.matmul(
            p[0:128, 0:NH], s_tri[:], r, start=True, stop=True),
            reads=["s_tri", "s_da"], writes=[PB[bc % 2]], name=f"cs0_{bc}")
        sch.add("pe", lambda eng, p=b1, r=daf(t0): eng.matmul(
            p[0:128, 0:NH], s_onessq[:], r, start=True, stop=False),
            reads=["s_onessq", "s_da"], writes=[PB[2 + bc % 2]], name=f"cs1a_{bc}")
        sch.add("pe", lambda eng, p=b1, r=daf(t1): eng.matmul(
            p[0:128, 0:NH], s_tri[:], r, start=False, stop=True),
            reads=["s_tri", "s_da"], writes=[PB[2 + bc % 2]], name=f"cs1b_{bc}")
        sch.add("pe", lambda eng, p=br, r=daf(t0): eng.matmul(
            p[0:1, 0:NH], s_onessq[:, 0:1], r, start=True, stop=False),
            reads=["s_onessq", "s_da"], writes=[PB[4 + bc % 2]], name=f"csr_{bc}")
        sch.add("pe", lambda eng, p=br, r=daf(t1): eng.matmul(
            p[0:1, 0:NH], s_onessq[:, 0:1], r, start=False, stop=True),
            reads=["s_onessq", "s_da"], writes=[PB[4 + bc % 2]], name=f"csr2_{bc}")
        sch.add("dve", lambda eng, p=b0, t=t0: eng.tensor_copy(
            s_dacs[:, t * NH:(t + 1) * NH], p[0:128, 0:NH]),
            reads=[PB[bc % 2]], writes=["s_dacs"], name=f"cse0_{bc}")
        sch.add("dve", lambda eng, p=b1, t=t1: eng.tensor_copy(
            s_dacs[:, t * NH:(t + 1) * NH], p[0:128, 0:NH]),
            reads=[PB[2 + bc % 2]], writes=["s_dacs"], name=f"cse1_{bc}")
        sch.add("dve", lambda eng, p=br, k=bc: eng.tensor_copy(
            s_dsumc[0:1, k * NH:(k + 1) * NH], p[0:1, 0:NH]),
            reads=[PB[4 + bc % 2]], writes=["s_dsumc"], name=f"cser_{bc}")
        sch.add("pe", lambda eng, p=bl, k=bc: eng.matmul(
            p[0:128, 0:NH], s_ones_row[:],
            s_dsumc[0:1, k * NH:(k + 1) * NH], start=True, stop=True),
            reads=["s_ones_row", "s_dsumc"], writes=[PB[6 + bc % 2]],
            name=f"csl_{bc}")
        sch.add("dve", lambda eng, p=bl, k=bc: eng.tensor_copy(
            s_dalast[:, k * NH:(k + 1) * NH], p[0:128, 0:NH]),
            reads=[PB[6 + bc % 2]], writes=["s_dalast"], name=f"csle_{bc}")
        sch.add("dve", lambda eng, t=t0, k=bc: eng.tensor_tensor(
            s_rowsc[:, t * NH:(t + 2) * NH], s_dacs[:, t * NH:(t + 2) * NH],
            rep_outer(s_dalast, NH, 2, col0=k * NH), ALU.subtract),
            reads=["s_dacs", "s_dalast"], writes=["s_rowsc"], name=f"df_{bc}")
        sch.add("act", lambda eng, t=t0: eng.activation(
            out=s_decay[:, t * NH:(t + 2) * NH],
            in_=s_rowsc[:, t * NH:(t + 2) * NH], func=AF.Exp, scale=-1.0),
            reads=["s_rowsc"], writes=["s_decay"], name=f"dk_{bc}")
        sch.add("act", lambda eng, t=t0: eng.activation(
            out=s_rowsc[:, t * NH:(t + 2) * NH],
            in_=s_rowsc[:, t * NH:(t + 2) * NH], func=AF.Exp),
            reads=["s_rowsc"], writes=["s_rowsc"], name=f"rs_{bc}")
        sch.add("act", lambda eng, k=bc: eng.activation(
            out=s_cds[:, k * NH:(k + 1) * NH],
            in_=s_dalast[:, k * NH:(k + 1) * NH], func=AF.Exp),
            reads=["s_dalast"], writes=["s_cds"], name=f"cd_{bc}")
    for bsel in range(B):
        c0 = bsel * 4
        for tcn in range(8):
            sch.add("pe", lambda eng, p=pb[bsel], t=c0 * 2 + tcn,
                    st=(tcn == 0), sp_=(tcn == 7): eng.matmul(
                        p[0:1, 0:NH], s_onessq[:, 0:1],
                        s_da[:, t * NH:(t + 1) * NH], start=st, stop=sp_),
                    reads=["s_onessq", "s_da"], writes=[PB[bsel]],
                    name=f"dsm{bsel}_{tcn}")
        sch.add("act", lambda eng, p=pb[bsel], b_=bsel: eng.copy(
            out=s_dsum[0:1, b_ * NH:(b_ + 1) * NH], in_=p[0:1, 0:NH]),
            reads=[PB[bsel]], writes=["s_dsum"], name=f"dse{bsel}")
    AR.pop()

    env = dict(locals())
    if stage < 3:
        return _finish(nc, sch, ctx, AR, dbg, env)

    AR.push()
    s_guardA2 = AR.alloc("s_guardA2", [128, 1024], F32)
    s_xtok = [AR.alloc(f"s_xtok{i}", [128, 2176], BF16) for i in range(2)]
    s_sdtx = [AR.alloc(f"s_sdtx{i}", [128, ZN], BF16) for i in range(2)]
    s_carry = [AR.alloc(f"s_carry{b_}", [128, 2048], F32) for b_ in range(B)]
    s_t1 = AR.alloc("s_t1", [128, 2048], F32)
    s_prevst = [AR.alloc(f"s_prevst{i}", [128, 2048], BF16) for i in range(2)]
    s_sc1 = AR.alloc("s_sc1", [128, 64], F32)

    XBCK = [f"xbc_f{e}" for e in range(17)]

    def load_xtok(tc, i, tag):
        inap = bass.AP(xbc_f.tensor, tc * 128, [[TMAIN, 2176], [1, 128]])
        sch.dma("sp", s_xtok[i][:], inap, XBCK, [f"s_xtok{i}"],
                f"tpx{tag}", transpose=True)

    def make_sdtx(tc, i, xl=s_xtok, sl=s_sdtx, cl=s_sc1):
        sch.add("dve", lambda eng, t=tc, j=i: eng.tensor_mul(
            cl[:, j * NH:(j + 1) * NH], s_dtsp[:, t * NH:(t + 1) * NH],
            s_decay[:, t * NH:(t + 1) * NH]),
            reads=["s_dtsp", "s_decay"], writes=["s_sc1"], name=f"sc{tc}")
        sch.add("dve", lambda eng, j=i: eng.tensor_tensor(
            sl[j][:], xl[j][:, 0:2048],
            rep_inner(cl, NH, HD, col0=j * NH), ALU.mult),
            reads=[f"s_xtok{i}", "s_sc1"], writes=[f"s_sdtx{i}"],
            name=f"sdtx{tc}")

    for bsel in range(B):
        for c in range(NCH):
            bc = bsel * NCH + c
            tcs = (2 * bc, 2 * bc + 1)
            for j, tc in enumerate(tcs):
                load_xtok(tc, j, f"p2_{tc}")
            sch.checkpoint("sp")
            for j, tc in enumerate(tcs):
                make_sdtx(tc, j)
            if debug and bc == 0:
                for j in range(2):
                    sch.dma("sp", dbg["dbg_xtok"][j * 128:(j + 1) * 128, :],
                            s_xtok[j][:], [f"s_xtok{j}"], ["dbg_xtok"],
                            f"dbgxt{j}")
                    sch.dma("sp", dbg["dbg_sdtx"][j * 128:(j + 1) * 128, :],
                            s_sdtx[j][:], [f"s_sdtx{j}"], ["dbg_sdtx"],
                            f"dbgsd{j}")
            qb = 0 if bc % 2 == 0 else 4
            for jc in range(2):
                for q in range(4):
                    sch.add("pe", lambda eng, p=pb[qb + q], j=jc, q_=q,
                            xl=s_xtok, sl=s_sdtx:
                            eng.matmul(p[:], xl[j][:, 2048:2176],
                                       sl[j][:, q_ * 512:(q_ + 1) * 512],
                                       start=(j == 0), stop=(j == 1)),
                            reads=[f"s_xtok{jc}", f"s_sdtx{jc}"],
                            writes=[PB[qb + q]], name=f"st{bc}_{jc}_{q}")
            ck, cn = s_carry[bsel], f"s_carry{bsel}"
            if c == 0:
                for q in range(4):
                    sch.add("dve", lambda eng, p=pb[qb + q], q_=q, c_=ck:
                            eng.tensor_copy(c_[:, q_ * 512:(q_ + 1) * 512], p[:]),
                            reads=[PB[qb + q]], writes=[cn], name=f"sc0e{bc}_{q}")
                if debug and bc == 0:
                    sch.dma("sp", dbg["dbg_st0"], ck[:], [cn], ["dbg_st0"],
                            "dbgst0")
            else:
                sch.add("dve", lambda eng, k=bc, c_=ck: eng.tensor_tensor(
                    s_t1[:], c_[:], rep_inner(s_cds, NH, HD, col0=k * NH),
                    ALU.mult),
                    reads=[cn, "s_cds"], writes=["s_t1"], name=f"t1_{bc}")
                i = bc % 2
                sch.add("pool", lambda eng, j=i: eng.tensor_copy(
                    s_prevst[j][:], s_t1[:]),
                    reads=["s_t1"], writes=[f"s_prevst{i}"], name=f"pvc{bc}")
                sch.dma("sp", prevsc_d[(bsel * 3 + c - 1) * 128:
                                       (bsel * 3 + c) * 128, :],
                        s_prevst[i][:], [f"s_prevst{i}"], ["prevsc_d"],
                        f"st_pv{bc}")
                for q in range(4):
                    sch.add("dve", lambda eng, p=pb[qb + q], q_=q, c_=ck:
                            eng.tensor_add(c_[:, q_ * 512:(q_ + 1) * 512],
                                           s_t1[:, q_ * 512:(q_ + 1) * 512],
                                           p[:]),
                            reads=[PB[qb + q], "s_t1"], writes=[cn],
                            name=f"scu{bc}_{q}")
        sch.dma("sp", cc_in[bsel * 128:(bsel + 1) * 128, :], s_carry[bsel][:],
                [f"s_carry{bsel}"], ["cc_in"], f"st_S{bsel}")
    for bsel in range(B):
        sch.dma("sp", cc_in[B * 128 + bsel:B * 128 + bsel + 1, 0:NH],
                s_dsum[0:1, bsel * NH:(bsel + 1) * NH], ["s_dsum"], ["cc_in"],
                f"st_ds{bsel}", full128=False)
    sch.checkpoint("sp")
    AR.pop()
    sch.add("pool", lambda eng: eng.collective_compute(
        "AllGather", ALU.bypass, replica_groups=[list(range(NCORES))],
        ins=[cc_in_h.ap().opt()], outs=[cc_out_h.ap().opt()]),
        reads=["cc_in"], writes=["cc_out"], kind="cc", name="allgather")

    env = dict(locals())
    if stage < 4:
        return _finish(nc, sch, ctx, AR, dbg, env)

    AR.push()
    AR.push()
    s_wg = {}
    for g in ("wf", "wi", "wc"):
        s_wg[g] = [AR.alloc(f"s_{g}{d_}", [128, D], BF16) for d_ in range(8)]
        for dc in range(8):
            sch.dma("pool", s_wg[g][dc][:], wffn[g][dc * 128:(dc + 1) * 128, :],
                    [g + "_t"], [f"s_{g}{dc}"], f"ld_{g}{dc}")
        sch.checkpoint("pool")
    s_ug = [AR.alloc(f"s_ug{d_}", [128, D], BF16) for d_ in range(8)]
    s_ug2 = [AR.alloc(f"s_ug2{d_}", [128, D], BF16) for d_ in range(8)]
    s_rrow = AR.alloc("s_rrow", [1, D], F32)
    s_c1 = [AR.alloc(f"s_c1_{m}", [128, 512], F32) for m in range(8)]
    s_c1b = [AR.alloc(f"s_c1b{m}", [128, 512], BF16) for m in range(8)]
    s_c2a = [AR.alloc(f"s_c2a{m}", [128, 512], F32) for m in range(8)]
    s_c2st = AR.alloc("s_c2st", [128, 512], BF16)
    s_gt = [AR.alloc(f"s_gt{i}", [128, 512], F32) for i in range(2)]

    for gi, g in enumerate(("uf", "ui", "uc")):
        for cc_ in range(8):
            sch.dma("pool", s_ug[cc_][:], wffn[g][cc_ * 128:(cc_ + 1) * 128, :],
                    [g + "_t"], [f"s_ug{cc_}"], f"mv_ld_{g}{cc_}")
        sch.checkpoint("pool")
        for hv in range(2):
            for cc_ in range(8):
                sch.add("pe", lambda eng, p=pb[hv], c_=cc_, h_=hv: eng.matmul(
                    p[0:1, 0:512], s_incol_bf[:, c_:c_ + 1],
                    s_ug[c_][:, h_ * 512:(h_ + 1) * 512],
                    start=(c_ == 0), stop=(c_ == 7)),
                    reads=["s_incol_bf", f"s_ug{cc_}"], writes=[PB[hv]],
                    name=f"mv{g}_{hv}_{cc_}")
            sch.add("dve", lambda eng, p=pb[hv], h_=hv: eng.tensor_copy(
                s_rrow[0:1, h_ * 512:(h_ + 1) * 512], p[0:1, 0:512]),
                reads=[PB[hv]], writes=["s_rrow"], name=f"mve{g}_{hv}")
        for mc in range(8):
            bank = 2 + mc % 2
            sch.add("pe", lambda eng, p=pb[bank], m=mc: eng.matmul(
                p[0:128, 0:1], s_rrow[0:1, m * 128:(m + 1) * 128],
                s_ones_row[0:1, 0:1], start=True, stop=True),
                reads=["s_rrow", "s_ones_row"], writes=[PB[bank]],
                name=f"mvc{g}_{mc}")
            col = gi * 8 + mc
            sch.add("dve", lambda eng, p=pb[bank], c_=col: eng.tensor_add(
                s_biascol[:, c_:c_ + 1], p[0:128, 0:1], s_gbias[:, c_:c_ + 1]),
                reads=[PB[bank], "s_gbias"], writes=["s_biascol"],
                name=f"mvb{g}_{mc}")

    GSIG = {"wf": (AF.Sigmoid, 0), "wi": (AF.Sigmoid, 1), "wc": (AF.Tanh, 2)}
    for lh in range(2):
        xcol0 = 3 + lh * 512
        for mc in range(8):
            banks = {"wf": 0, "wi": 1, "wc": 2}
            if mc % 2:
                banks = {k: v + 4 for k, v in banks.items()}
            for g in ("wf", "wi", "wc"):
                bk = banks[g]
                for dc in range(8):
                    sch.add("pe", lambda eng, p=pb[bk], g_=g, d_=dc, m=mc,
                            x0=xcol0:
                            eng.matmul(p[:], s_wg[g_][d_][:, m * 128:(m + 1) * 128],
                                       xnfm[d_][:, x0:x0 + 512],
                                       start=(d_ == 0), stop=(d_ == 7)),
                            reads=[f"s_{g}{dc}", XN[dc]], writes=[PB[bk]],
                            name=f"s1{g}{lh}_{mc}_{dc}")
            func_f, _ = GSIG["wf"]
            sch.add("act", lambda eng, p=pb[banks['wf']], m=mc: eng.activation(
                out=s_gt[0][:], in_=p[:], func=AF.Sigmoid,
                bias=s_biascol[:, 0 + m:0 + m + 1]),
                reads=[PB[banks["wf"]]], scalar_reads=["s_biascol"],
                writes=["s_gt0"], name=f"s1ef{lh}_{mc}")
            sch.add("act", lambda eng, p=pb[banks['wi']], m=mc: eng.activation(
                out=s_gt[1][:], in_=p[:], func=AF.Sigmoid,
                bias=s_biascol[:, 8 + m:8 + m + 1]),
                reads=[PB[banks["wi"]]], scalar_reads=["s_biascol"],
                writes=["s_gt1"], name=f"s1ei{lh}_{mc}")
            sch.add("act", lambda eng, p=pb[banks['wc']], m=mc: eng.activation(
                out=s_c1[m][:], in_=p[:], func=AF.Tanh,
                bias=s_biascol[:, 16 + m:16 + m + 1]),
                reads=[PB[banks["wc"]]], scalar_reads=["s_biascol"],
                writes=[f"s_c1_{mc}"], name=f"s1ec{lh}_{mc}")
            sch.add("dve", lambda eng, m=mc: eng.tensor_mul(
                s_c1[m][:], s_gt[1][:], s_c1[m][:]),
                reads=["s_gt1", f"s_c1_{mc}"], writes=[f"s_c1_{mc}"],
                name=f"s1m{lh}_{mc}")
            sch.add("dve", lambda eng, m=mc: eng.scalar_tensor_tensor(
                s_c1[m][:], s_gt[0][:], s_incol[:, m:m + 1], s_c1[m][:],
                ALU.mult, ALU.add),
                reads=["s_gt0", f"s_c1_{mc}"], scalar_reads=["s_incol"],
                writes=[f"s_c1_{mc}"], name=f"s1c{lh}_{mc}")
            sch.add("pool", lambda eng, m=mc: eng.tensor_copy(
                s_c1b[m][:], s_c1[m][:]),
                reads=[f"s_c1_{mc}"], writes=[f"s_c1b{mc}"], name=f"s1b{lh}_{mc}")
        for gsl, g in ((s_ug, "uc"), (s_ug2, "ui")):
            for cc_ in range(8):
                key = "s_ug" if gsl is s_ug else "s_ug2"
                sch.dma("pool", gsl[cc_][:], wffn[g][cc_ * 128:(cc_ + 1) * 128, :],
                        [g + "_t"], [f"{key}{cc_}"], f"s2ld_{g}{cc_}_{lh}")
            sch.checkpoint("pool")
        xb1 = TPB + 3 + lh * 512
        for mc in range(8):
            bkc = 0 + 2 * (mc % 2)
            bki = 1 + 2 * (mc % 2)
            for bk, wname, ut, ukey in ((bkc, "wc", s_ug, "s_ug"),
                                        (bki, "wi", s_ug2, "s_ug2")):
                for dc in range(8):
                    sch.add("pe", lambda eng, p=pb[bk], w=wname, d_=dc, m=mc,
                            x0=xb1:
                            eng.matmul(p[:], s_wg[w][d_][:, m * 128:(m + 1) * 128],
                                       xnfm[d_][:, x0:x0 + 512],
                                       start=(d_ == 0), stop=False),
                            reads=[f"s_{wname}{dc}", XN[dc]], writes=[PB[bk]],
                            name=f"s2x{wname}{lh}_{mc}_{dc}")
                for cc_ in range(8):
                    sch.add("pe", lambda eng, p=pb[bk], t=ut, c_=cc_, m=mc:
                            eng.matmul(p[:], t[c_][:, m * 128:(m + 1) * 128],
                                       s_c1b[c_][:], start=False, stop=(c_ == 7)),
                            reads=[f"{ukey}{cc_}", f"s_c1b{cc_}"], writes=[PB[bk]],
                            name=f"s2c{wname}{lh}_{mc}_{cc_}")
            sch.add("act", lambda eng, p=pb[bkc], m=mc: eng.activation(
                out=s_gt[0][:], in_=p[:], func=AF.Tanh,
                bias=s_gbias[:, 16 + m:16 + m + 1]),
                reads=[PB[bkc]], scalar_reads=["s_gbias"], writes=["s_gt0"],
                name=f"s2ec{lh}_{mc}")
            sch.add("act", lambda eng, p=pb[bki], m=mc: eng.activation(
                out=s_gt[1][:], in_=p[:], func=AF.Sigmoid,
                bias=s_gbias[:, 8 + m:8 + m + 1]),
                reads=[PB[bki]], scalar_reads=["s_gbias"], writes=["s_gt1"],
                name=f"s2ei{lh}_{mc}")
            sch.add("dve", lambda eng, m=mc: eng.tensor_mul(
                s_c2a[m][:], s_gt[0][:], s_gt[1][:]),
                reads=["s_gt0", "s_gt1"], writes=[f"s_c2a{mc}"],
                name=f"s2m{lh}_{mc}")
        for cc_ in range(8):
            sch.dma("pool", s_ug[cc_][:], wffn["uf"][cc_ * 128:(cc_ + 1) * 128, :],
                    ["uf_t"], [f"s_ug{cc_}"], f"s2ld_uf{cc_}_{lh}")
        sch.checkpoint("pool")
        for mc in range(8):
            bk = 0 + mc % 2
            for dc in range(8):
                sch.add("pe", lambda eng, p=pb[bk], d_=dc, m=mc, x0=xb1:
                        eng.matmul(p[:], s_wg["wf"][d_][:, m * 128:(m + 1) * 128],
                                   xnfm[d_][:, x0:x0 + 512],
                                   start=(d_ == 0), stop=False),
                        reads=[f"s_wf{dc}", XN[dc]], writes=[PB[bk]],
                        name=f"s2xf{lh}_{mc}_{dc}")
            for cc_ in range(8):
                sch.add("pe", lambda eng, p=pb[bk], c_=cc_, m=mc:
                        eng.matmul(p[:], s_ug[c_][:, m * 128:(m + 1) * 128],
                                   s_c1b[c_][:], start=False, stop=(c_ == 7)),
                        reads=[f"s_ug{cc_}", f"s_c1b{cc_}"], writes=[PB[bk]],
                        name=f"s2cf{lh}_{mc}_{cc_}")
            sch.add("act", lambda eng, p=pb[bk], m=mc: eng.activation(
                out=s_gt[0][:], in_=p[:], func=AF.Sigmoid,
                bias=s_gbias[:, 0 + m:0 + m + 1]),
                reads=[PB[bk]], scalar_reads=["s_gbias"], writes=["s_gt0"],
                name=f"s2ef{lh}_{mc}")
            sch.add("dve", lambda eng, m=mc: eng.tensor_mul(
                s_gt[1][:], s_gt[0][:], s_c1[m][:]),
                reads=["s_gt0", f"s_c1_{mc}"], writes=["s_gt1"],
                name=f"s2fm{lh}_{mc}")
            sch.add("dve", lambda eng, m=mc: eng.tensor_add(
                s_c2a[m][:], s_c2a[m][:], s_gt[1][:]),
                reads=["s_gt1", f"s_c2a{mc}"], writes=[f"s_c2a{mc}"],
                name=f"s2fa{lh}_{mc}")
            sch.add("pool", lambda eng, m=mc: eng.tensor_copy(
                s_c2st[:], s_c2a[m][:]),
                reads=[f"s_c2a{mc}"], writes=["s_c2st"], name=f"s2b{lh}_{mc}")
            sch.dma("sp", c2_d[mc * 128:(mc + 1) * 128, lh * 512:(lh + 1) * 512],
                    s_c2st[:], ["s_c2st"], ["c2_d"], f"st_c2{lh}_{mc}")
    AR.pop()

    AR.push()
    s_wo = [AR.alloc(f"s_wo{d_}", [128, D], BF16) for d_ in range(8)]
    s_uo = [AR.alloc(f"s_uo{d_}", [128, D], BF16) for d_ in range(8)]
    s_c2b = [AR.alloc(f"s_c2bb{m}", [128, LLOC], BF16) for m in range(8)]
    for mc in range(8):
        sch.dma("sp", s_c2b[mc][:], c2_d[mc * 128:(mc + 1) * 128, :],
                ["c2_d"], [f"s_c2b{mc}"], f"ld_c2{mc}")
    for dc in range(8):
        sch.dma("pool", s_wo[dc][:], wffn["wo"][dc * 128:(dc + 1) * 128, :],
                ["wo_t"], [f"s_wo{dc}"], f"ld_wo{dc}")
        sch.dma("pool", s_uo[dc][:], wffn["uo"][dc * 128:(dc + 1) * 128, :],
                ["uo_t"], [f"s_uo{dc}"], f"ld_uo{dc}")
    sch.checkpoint("pool")
    s_U = [AR.alloc(f"s_U{l}", [128, D], F32) for l in range(8)]
    s_ht = [AR.alloc(f"s_ht{i}", [128, D], F32) for i in range(2)]
    s_ht2 = [AR.alloc(f"s_ht2{i}", [128, D], F32) for i in range(2)]
    for lc in range(8):
        bks = (0, 1) if lc % 2 == 0 else (2, 3)
        for q in range(2):
            for cc_ in range(8):
                sch.add("pe", lambda eng, p=pb[bks[q]], c_=cc_, l=lc, q_=q:
                        eng.matmul(p[:], s_c2b[c_][:, l * 128:(l + 1) * 128],
                                   s_uo[c_][:, q_ * 512:(q_ + 1) * 512],
                                   start=(c_ == 0), stop=(c_ == 7)),
                        reads=[f"s_c2b{cc_}", f"s_uo{cc_}"], writes=[PB[bks[q]]],
                        name=f"U{lc}_{q}_{cc_}")
            sch.add("dve", lambda eng, p=pb[bks[q]], l=lc, q_=q: eng.tensor_add(
                s_U[l][:, q_ * 512:(q_ + 1) * 512], p[:],
                s_bo_bc[:, q_ * 512:(q_ + 1) * 512]),
                reads=[PB[bks[q]], "s_bo_bc"], writes=[f"s_U{lc}"],
                name=f"Ue{lc}_{q}")
    for bsel in range(B):
        for lc in range(8):
            i = lc % 2
            bks = (4, 5) if i == 0 else (6, 7)
            tcn = bsel * 8 + lc
            for q in range(2):
                for dc in range(8):
                    sch.add("pe", lambda eng, p=pb[bks[q]], d_=dc, t=tcn, q_=q:
                            eng.matmul(p[:], xnfm[d_][:, tcol(t):tcol(t) + 128],
                                       s_wo[d_][:, q_ * 512:(q_ + 1) * 512],
                                       start=(d_ == 0), stop=(d_ == 7)),
                            reads=[XN[dc], f"s_wo{dc}"], writes=[PB[bks[q]]],
                            name=f"h{tcn}_{q}_{dc}")
                sch.add("dve", lambda eng, p=pb[bks[q]], l=lc, j=i, q_=q:
                        eng.tensor_add(s_ht[j][:, q_ * 512:(q_ + 1) * 512],
                                       p[:], s_U[l][:, q_ * 512:(q_ + 1) * 512]),
                        reads=[PB[bks[q]], f"s_U{lc}"], writes=[f"s_ht{i}"],
                        name=f"he{tcn}_{q}")
            sch.add("act", lambda eng, j=i: eng.activation(
                out=s_ht2[j][:], in_=s_ht[j][:], func=AF.Tanh),
                reads=[f"s_ht{i}"], writes=[f"s_ht2{i}"], name=f"ht{tcn}")
            sch.dma("act", h_t[tcn * 128:(tcn + 1) * 128, :], s_ht2[i][:],
                    [f"s_ht2{i}"], ["h_t"], f"st_h{tcn}")
    AR.pop()
    AR.pop()
    sch.barrier()

    env = dict(locals())
    if stage < 5:
        return _finish(nc, sch, ctx, AR, dbg, env)

    AR.push()
    s_nwrow = AR.alloc("s_nwrow", [1, DIN], F32)
    s_nwbc = AR.alloc("s_nwbc", [128, DIN], F32)
    s_cin = [AR.alloc(f"s_cin{b_}", [128, 2048], F32) for b_ in range(B)]
    s_stile = [AR.alloc(f"s_stile{i}", [128, 2048], F32) for i in range(2)]
    s_t2 = AR.alloc("s_t2", [128, 2048], F32)
    s_dgm = AR.alloc("s_dgm", [9, 64], F32)
    s_wsm = AR.alloc("s_wsm", [8, 64], F32)
    s_wline = AR.alloc("s_wline", [1, 512], F32)
    s_wbc = AR.alloc("s_wbc", [128, NH], F32)
    s_hcds = AR.alloc("s_hcds", [128, 8 * NH], F32)
    s_cing = [AR.alloc(f"s_cing{i}", [128, 2048], BF16) for i in range(2)]
    s_xtokB = [AR.alloc(f"s_xtokB{i}", [128, 2176], BF16) for i in range(2)]
    s_sdtxB = [AR.alloc(f"s_sdtxB{i}", [128, ZN], BF16) for i in range(2)]
    s_sc1B = AR.alloc("s_sc1B", [128, 64], F32)
    s_dx = [AR.alloc(f"s_dx{i}", [128, 2048], F32) for i in range(2)]
    s_zrd = [AR.alloc(f"s_zrd{i}", [128, ZN], BF16) for i in range(2)]
    s_prd = [AR.alloc(f"s_prd{i}", [128, 2048], BF16) for i in range(2)]
    s_cbt0 = AR.alloc("s_cbt0", [128, 256], BF16)
    s_cbt1 = AR.alloc("s_cbt1", [128, 128], BF16)
    s_y = [AR.alloc(f"s_y{i}", [128, 2048], F32) for i in range(2)]
    s_gbf = [AR.alloc(f"s_gbf{i}", [128, 2048], BF16) for i in range(2)]
    s_rcol = AR.alloc("s_rcol", [128, 32], F32)

    sch.dma("sp", s_nwrow[:], norm_w_row, [], ["s_nwrow"], "ld_nw",
            full128=False)
    sch.checkpoint("sp")
    for q in range(4):
        bcast_row(s_nwrow[0:1, q * 512:(q + 1) * 512], "s_nwrow",
                  s_nwbc[:, q * 512:(q + 1) * 512], "s_nwbc", 512, q, f"nw{q}")

    for m in range(8):
        for bsel in range(B):
            sch.dma("sp", s_dgm[m + 1:m + 2, bsel * NH:(bsel + 1) * NH],
                    cc_out[m * CCROWS + B * 128 + bsel:
                           m * CCROWS + B * 128 + bsel + 1, 0:NH],
                    ["cc_out"], ["s_dgm"], f"ld_dg{m}_{bsel}", full128=False)
    sch.checkpoint("sp")
    sch.add("pool", lambda eng: eng.memset(s_dgm[0:1, :], 1.0),
            reads=[], writes=["s_dgm"], name="dgm_ones")
    sch.add("pe", lambda eng: eng.matmul(pb[4][0:8, 0:64], s_m2t[:], s_dgm[:],
                                         start=True, stop=True),
            reads=["s_m2t", "s_dgm"], writes=[PB[4]], name="wmm")
    sch.add("act", lambda eng: eng.activation(out=s_wsm[:], in_=pb[4][0:8, 0:64],
                                              func=AF.Exp),
            reads=[PB[4]], writes=["s_wsm"], name="wexp")
    sch.dma("sp", w_d, s_wsm[:], ["s_wsm"], ["w_d"], "st_wd", full128=False)
    sch.checkpoint("sp")
    for j in range(7):
        for bsel in range(B):
            jb = j * 2 + bsel
            sch.dma("sp", s_wline[0:1, jb * NH:(jb + 1) * NH],
                    w_d[j:j + 1, bsel * NH:(bsel + 1) * NH], ["w_d"],
                    ["s_wline"], f"ld_wl{jb}", full128=False)
    sch.checkpoint("sp")
    for bsel in range(B):
        sch.add("pool", lambda eng, b_=bsel: eng.memset(s_cin[b_][:], 0.0),
                reads=[], writes=[f"s_cin{bsel}"], name=f"cin0_{bsel}")
    for j in range(7):
        for bsel in range(B):
            jb = j * 2 + bsel
            i = jb % 2
            sch.dma("act", s_stile[i][:],
                    cc_out[j * CCROWS + bsel * 128:j * CCROWS + (bsel + 1) * 128, :],
                    ["cc_out"], [f"s_stile{i}"], f"ld_S{jb}")
            sch.add("pe", lambda eng, c_=jb: eng.matmul(
                pb[5][0:128, 0:NH], s_ones_row[:],
                s_wline[0:1, c_ * NH:(c_ + 1) * NH], start=True, stop=True),
                reads=["s_ones_row", "s_wline"], writes=[PB[5]], name=f"wbc{jb}")
            sch.add("dve", lambda eng: eng.tensor_copy(s_wbc[:], pb[5][0:128, 0:NH]),
                    reads=[PB[5]], writes=["s_wbc"], name=f"wbce{jb}")
            sch.add("pool", lambda eng, j_=i: eng.tensor_tensor(
                s_t2[:], s_stile[j_][:], rep_inner(s_wbc, NH, HD), ALU.mult),
                reads=[f"s_stile{i}", "s_wbc"], writes=["s_t2"], name=f"cmb{jb}")
            sch.add("pool", lambda eng, b_=bsel: eng.tensor_add(
                s_cin[b_][:], s_cin[b_][:], s_t2[:]),
                reads=["s_t2", f"s_cin{bsel}"], writes=[f"s_cin{bsel}"],
                name=f"cma{jb}")
    if debug:
        for b_ in range(B):
            sch.dma("sp", dbg["dbg_cin"][b_ * 128:(b_ + 1) * 128, :],
                    s_cin[b_][:], [f"s_cin{b_}"], ["dbg_cin"], f"dmp_cin{b_}")
    for bsel in range(B):
        k0 = bsel * 4
        sch.add("dve", lambda eng, k=k0: eng.tensor_copy(
            s_hcds[:, k * NH:(k + 1) * NH], s_cds[:, k * NH:(k + 1) * NH]),
            reads=["s_cds"], writes=["s_hcds"], name=f"hc0_{bsel}")
        for c in range(1, 4):
            gap("dve")
            gap("dve")
            sch.add("dve", lambda eng, k=k0 + c: eng.tensor_mul(
                s_hcds[:, k * NH:(k + 1) * NH],
                s_hcds[:, (k - 1) * NH:k * NH], s_cds[:, k * NH:(k + 1) * NH]),
                reads=["s_cds", "s_hcds"], writes=["s_hcds"], name=f"hc{bsel}_{c}")
    if debug:
        sch.add("pool", lambda eng: eng.tensor_copy(s_wbc[:], s_wbc[:]),
                reads=["s_hcds"], writes=[], name="hcfence")
        sch.dma("sp", dbg["dbg_small"][0:128, 832:832 + 256].rearrange(
            "p c -> p c") if False else dbg["dbg_small"][0:128, 768:1024],
            s_hcds[:], ["s_hcds"], ["dbg_small"], "dmp_hc")

    for bsel in range(B):
        for c in range(NCH):
            bc = bsel * NCH + c
            tcs = (2 * bc, 2 * bc + 1)
            ic_rot = bc % 2
            for j, tc in enumerate(tcs):
                inap = bass.AP(xbc_f.tensor, tc * 128, [[TMAIN, 2176], [1, 128]])
                sch.dma("sp", s_xtokB[j][:], inap,
                        [f"xbc_f{e}" for e in range(17)], [f"s_xtokB{j}"],
                        f"tpxB{tc}", transpose=True)
                sch.dma("act", s_zrd[j][:], z_t[tc * 128:(tc + 1) * 128, :],
                        [f"z_t{tc}"], [f"s_zrd{j}"], f"ldz{tc}")
            if c > 0:
                sch.dma("act", s_prd[ic_rot][:],
                        prevsc_d[(bsel * 3 + c - 1) * 128:(bsel * 3 + c) * 128, :],
                        ["prevsc_d"], [f"s_prd{ic_rot}"], f"ldpv{bc}")
            sch.checkpoint("sp")
            sch.checkpoint("act")
            for j, tc in enumerate(tcs):
                sch.add("dve", lambda eng, t=tc, j_=j: eng.tensor_mul(
                    s_sc1B[:, j_ * NH:(j_ + 1) * NH],
                    s_dtsp[:, t * NH:(t + 1) * NH],
                    s_decay[:, t * NH:(t + 1) * NH]),
                    reads=["s_dtsp", "s_decay"], writes=["s_sc1B"], name=f"scB{tc}")
                sch.add("dve", lambda eng, j_=j: eng.tensor_tensor(
                    s_sdtxB[j_][:], s_xtokB[j_][:, 0:2048],
                    rep_inner(s_sc1B, NH, HD, col0=j_ * NH), ALU.mult),
                    reads=[f"s_xtokB{j}", "s_sc1B"], writes=[f"s_sdtxB{j}"],
                    name=f"sdtxB{tc}")
                sch.add("pool", lambda eng, j_=j: eng.tensor_tensor(
                    s_dx[j_][:], s_xtokB[j_][:, 0:2048],
                    rep_inner(s_d_bc, NH, HD), ALU.mult),
                    reads=[f"s_xtokB{j}", "s_d_bc"], writes=[f"s_dx{j}"],
                    name=f"dxB{tc}")
            sch.add("pool", lambda eng, b_=bsel, k=bc, j_=ic_rot:
                    eng.tensor_tensor(s_cing[j_][:], s_cin[b_][:],
                                      rep_inner(s_hcds, NH, HD, col0=k * NH),
                                      ALU.mult),
                    reads=[f"s_cin{bsel}", "s_hcds"], writes=[f"s_cing{ic_rot}"],
                    name=f"cing{bc}")
            tb0 = bsel * 1024 + c * 256
            sch.add("pe", lambda eng, t0=tb0: eng.matmul(
                pb[0][0:128, 0:256], s_Bfm[:, t0:t0 + 128],
                s_Cfm[:, t0:t0 + 256], start=True, stop=True),
                reads=["s_Bfm", "s_Cfm"], writes=[PB[0]], name=f"cbt0_{bc}")
            sch.add("pe", lambda eng, t0=tb0: eng.matmul(
                pb[1][0:128, 0:128], s_Bfm[:, t0 + 128:t0 + 256],
                s_Cfm[:, t0 + 128:t0 + 256], start=True, stop=True),
                reads=["s_Bfm", "s_Cfm"], writes=[PB[1]], name=f"cbt1_{bc}")
            sch.add("dve", lambda eng: eng.tensor_tensor(
                s_cbt0[:], pb[0][0:128, 0:256], s_mask256[:], ALU.mult),
                reads=[PB[0], "s_mask256"], writes=["s_cbt0"], name=f"cbtm0_{bc}")
            sch.add("dve", lambda eng: eng.tensor_tensor(
                s_cbt1[:], pb[1][0:128, 0:128], s_tri[:], ALU.mult),
                reads=[PB[1], "s_tri"], writes=["s_cbt1"], name=f"cbtm1_{bc}")
            for ic in range(2):
                tc = tcs[ic]
                qb = 0 if ic == 0 else 4
                for q in range(4):
                    bk = qb + q
                    srcs = []
                    srcs.append((s_cbt0[:, ic * 128:(ic + 1) * 128],
                                 "s_cbt0", s_sdtxB[0], "s_sdtxB0"))
                    if ic == 1:
                        srcs.append((s_cbt1[:, 0:128], "s_cbt1",
                                     s_sdtxB[1], "s_sdtxB1"))
                    cslice = s_Cfm[:, tb0 + ic * 128:tb0 + (ic + 1) * 128]
                    if c > 0:
                        srcs.append((cslice, "s_Cfm", s_prd[ic_rot],
                                     f"s_prd{ic_rot}"))
                    srcs.append((cslice, "s_Cfm", s_cing[ic_rot],
                                 f"s_cing{ic_rot}"))
                    n = len(srcs)
                    for si, (lh_, lk, rt, rk) in enumerate(srcs):
                        sch.add("pe", lambda eng, p=pb[bk], l=lh_, r=rt, q_=q,
                                st=(si == 0), sp_=(si == n - 1): eng.matmul(
                                    p[:], l, r[:, q_ * 512:(q_ + 1) * 512],
                                    start=st, stop=sp_),
                                reads=[lk, rk], writes=[PB[bk]],
                                name=f"Y{bc}_{ic}_{q}_{si}")
                yt, ytk = s_y[ic], f"s_y{ic}"
                for q in range(4):
                    sch.add("dve", lambda eng, p=pb[qb + q], t=tc, q_=q, y=yt:
                            eng.tensor_tensor(
                                y[:, q_ * 512:(q_ + 1) * 512], p[:],
                                rep_inner(s_rowsc, 8, HD, col0=t * NH + q_ * 8),
                                ALU.mult),
                            reads=[PB[qb + q], "s_rowsc"], writes=[ytk],
                            name=f"ye{bc}_{ic}_{q}")
                sch.add("pool", lambda eng, j_=ic, y=yt: eng.tensor_add(
                    y[:], y[:], s_dx[j_][:]),
                    reads=[ytk, f"s_dx{ic}"], writes=[ytk], name=f"yd{bc}_{ic}")
                sch.add("dve", lambda eng, j_=ic, y=yt: eng.tensor_mul(
                    y[:], y[:], s_zrd[j_][:]),
                    reads=[ytk, f"s_zrd{ic}"], writes=[ytk], name=f"yz{bc}_{ic}")
                co = 16 * ic
                sch.add("act", lambda eng, j_=ic, y=yt: eng.activation(
                    out=s_dx[j_][:], in_=y[:], func=AF.Square),
                    reads=[ytk], writes=[f"s_dx{ic}"], name=f"ysq{bc}_{ic}")
                sch.add("dve", lambda eng, j_=ic, c_=co: eng.tensor_reduce(
                    s_rcol[:, c_:c_ + 1], s_dx[j_][:], AX.X, ALU.add),
                    reads=[f"s_dx{ic}"], writes=[f"s_rcolq{ic}"],
                    name=f"yrd{bc}_{ic}")
                sch.add("act", lambda eng, c_=co: eng.activation(
                    out=s_rcol[:, c_ + 4:c_ + 5], in_=s_rcol[:, c_:c_ + 1],
                    func=AF.Ln, scale=1.0 / DIN, bias=1e-5),
                    reads=[f"s_rcolq{ic}", "s_constrow"],
                    writes=[f"s_rcoll{ic}"], name=f"yln{bc}_{ic}")
                gap("act")
                sch.add("act", lambda eng, c_=co: eng.activation(
                    out=s_rcol[:, c_ + 8:c_ + 9], in_=s_rcol[:, c_ + 4:c_ + 5],
                    func=AF.Exp, scale=-0.5),
                    reads=[f"s_rcoll{ic}"], writes=[f"s_rcolr{ic}"],
                    name=f"yrs{bc}_{ic}")
                sch.add("dve", lambda eng, j_=ic, c_=co, y=yt: eng.scalar_tensor_tensor(
                    s_gbf[j_][:], y[:], s_rcol[:, c_ + 8:c_ + 9],
                    s_nwbc[:], ALU.mult, ALU.mult),
                    reads=[ytk, "s_nwbc"], scalar_reads=[f"s_rcolr{ic}"],
                    writes=[f"s_gbf{ic}"], name=f"ygn{bc}_{ic}")
                sch.dma("sp", g_t[tc * 128:(tc + 1) * 128, :], s_gbf[ic][:],
                        [f"s_gbf{ic}"], ["g_t"], f"st_g{tc}")
    AR.pop()
    sch.barrier()

    env = dict(locals())
    if stage < 6:
        return _finish(nc, sch, ctx, AR, dbg, env)

    AR.push()
    s_gfm = [AR.alloc(f"s_gfm{d_}", [128, TMAIN], BF16) for d_ in range(16)]
    s_wou = [AR.alloc(f"s_wou{d_}", [128, D], BF16) for d_ in range(16)]
    s_ur = [AR.alloc(f"s_urr{i}", [128, D], F32) for i in range(2)]
    s_hr = [AR.alloc(f"s_hrr{i}", [128, D], F32) for i in range(2)]
    s_o = [AR.alloc(f"s_o{i}", [128, D], F32) for i in range(2)]
    for dc in range(16):
        sch.dma("pool", s_wou[dc][:], w_out_t[dc * 128:(dc + 1) * 128, :],
                ["w_out_t"], [f"s_wou{dc}"], f"ld_wout{dc}")
    sch.checkpoint("pool")
    for dc in range(16):
        inap = bass.AP(g_t.tensor, dc * 128, [[DIN, TMAIN], [1, 128]])
        sch.dma("sp", s_gfm[dc][:], inap, ["g_t"], [f"s_gfm{dc}"],
                f"tp_g{dc}", transpose=True)
    sch.checkpoint("sp")
    for tc in range(NTC):
        bsel, r0, i = tc // 8, (tc % 8) * 128, tc % 2
        bks = (0, 1) if i == 0 else (2, 3)
        for q in range(2):
            for dc in range(16):
                sch.add("pe", lambda eng, p=pb[bks[q]], d_=dc, t=tc, q_=q:
                        eng.matmul(p[:], s_gfm[d_][:, t * 128:(t + 1) * 128],
                                   s_wou[d_][:, q_ * 512:(q_ + 1) * 512],
                                   start=(d_ == 0), stop=(d_ == 15)),
                        reads=[f"s_gfm{dc}", f"s_wou{dc}"], writes=[PB[bks[q]]],
                        name=f"op{tc}_{q}_{dc}")
        sch.dma("sp", s_ur[i][:], u[bsel, r0:r0 + 128, :], ["u"],
                [f"s_urr{i}"], f"ldu2_{tc}")
        sch.dma("act", s_hr[i][:], h_t[tc * 128:(tc + 1) * 128, :], ["h_t"],
                [f"s_hrr{i}"], f"ldh{tc}")
        for q in range(2):
            sch.add("dve", lambda eng, p=pb[bks[q]], j=i, q_=q: eng.tensor_add(
                s_o[j][:, q_ * 512:(q_ + 1) * 512],
                s_ur[j][:, q_ * 512:(q_ + 1) * 512], p[:]),
                reads=[PB[bks[q]], f"s_urr{i}"], writes=[f"s_o{i}"],
                name=f"oa{tc}_{q}")
        sch.add("pool", lambda eng, j=i: eng.tensor_add(
            s_o[j][:], s_o[j][:], s_hr[j][:]),
            reads=[f"s_hrr{i}", f"s_o{i}"], writes=[f"s_o{i}"], name=f"oh{tc}")
        sch.dma("act", out[bsel, r0:r0 + 128, :], s_o[i][:], [f"s_o{i}"],
                ["out"], f"st_o{tc}")
    AR.pop()

    env = dict(locals())
    return _finish(nc, sch, ctx, AR, dbg, env)


def _finish(nc, sch, ctx, AR, dbg, env):
    pairs = [("dbg_c2", "c2_d"),
             ("dbg_xn", "xn_t"), ("dbg_xbc", "xbc_f"), ("dbg_z", "z_t"),
             ("dbg_g", "g_t"), ("dbg_h", "h_t"), ("dbg_prevsc", "prevsc_d"),
             ("dbg_ccin", "cc_in"), ("dbg_ccout", "cc_out")]
    for dn, srcn in pairs:
        if dn in dbg and srcn in env:
            src = env[srcn]
            srcap = src if isinstance(src, bass.AP) else src.ap()
            sch.add("act", lambda eng, o=dbg[dn], i=srcap:
                    eng.dma_start(out=o, in_=i),
                    reads=[srcn], writes=[dn], kind="d",
                    name=f"dump_{dn}", full128=False)
    if "dbg_bias" in dbg and "s_biascol" in env:
        t = env["s_biascol"]
        sch.add("act", lambda eng, o=dbg["dbg_bias"], i=t[:]:
                eng.dma_start(out=o, in_=i),
                reads=["s_biascol"], writes=["dbg_bias"], kind="d",
                name="dmp_bias")
    if "dbg_small" in dbg:
        for col0, sbn, rows, cols in [(0, "s_cds", 128, 256),
                                      (256, "s_dalast", 128, 256),
                                      (512, "s_dsumc", 1, 256),
                                      (768, "s_dsum", 1, 64)]:
            if sbn in env:
                t = env[sbn]
                sch.add("act", lambda eng, o=dbg["dbg_small"][0:rows, col0:col0 + cols],
                        i=t[0:rows, 0:cols]: eng.dma_start(out=o, in_=i),
                        reads=[sbn], writes=["dbg_small"], kind="d",
                        name=f"dmpL_{sbn}", full128=False)
    for dn, sbn in [("dbg_dt", "s_dtsp"), ("dbg_dacs", "s_dacs"),
                    ("dbg_decay", "s_decay")]:
        if dn in dbg and sbn in env:
            t = env[sbn]
            sch.add("act", lambda eng, o=dbg[dn], i=t[:]:
                    eng.dma_start(out=o, in_=i),
                    reads=[sbn], writes=[dn], kind="d", name=f"dump_{dn}")
    sch.finalize_and_emit(ctx)
    ctx.close()
    return nc



def make_inputs(full):
    uF = np.ascontiguousarray(full["L0_attn_u"], dtype=np.float32)
    smalls = dict(
        w_in_t=np.ascontiguousarray(full["L0_attn_W_in"].T, dtype=np.float32),
        conv_w_fm=np.ascontiguousarray(
            np.asarray(full["L0_attn_conv_w"], np.float32)
            .reshape(18, 128, 4).transpose(1, 0, 2).reshape(128, 72)),
        conv_b_fm=np.ascontiguousarray(
            np.asarray(full["L0_attn_conv_b"], np.float32).reshape(18, 128).T),
        dt_bias_row=np.asarray(full["L0_attn_dt_bias"], np.float32).reshape(1, NH),
        a_log_row=np.asarray(full["L0_attn_A_log"], np.float32).reshape(1, NH),
        d_param_row=np.asarray(full["L0_attn_D_param"], np.float32).reshape(1, NH),
        norm_w_row=np.asarray(full["L0_attn_norm_w"], np.float32).reshape(1, DIN),
        in_c_col=np.ascontiguousarray(
            np.asarray(full["L0_ffn_in_c"], np.float32).reshape(8, 128).T),
        gbias_col=np.ascontiguousarray(np.concatenate(
            [np.asarray(full["L0_ffn_" + k], np.float32).reshape(8, 128).T
             for k in ("bf", "bi", "bc")], axis=1)),
        bo_row=np.asarray(full["L0_ffn_bo"], np.float32).reshape(1, D),
        ones_row=np.ones((1, 128), np.float32),
        consts_row=np.tile(np.array([[1e-5, 1.0, 0.0, 0.0]], np.float32),
                           (128, 1)),
        tri_f32=np.ascontiguousarray(np.triu(np.ones((128, 128), np.float32))),
        ones_sq=np.ones((128, 128), np.float32),
        mask256=np.ascontiguousarray(np.concatenate(
            [np.triu(np.ones((128, 128), np.float32)),
             np.ones((128, 128), np.float32)], axis=1)),
        w_out_t=np.ascontiguousarray(full["L0_attn_W_out"].T, dtype=np.float32),
    )
    for nm in ("wf", "uf", "wi", "ui", "wc", "uc", "wo", "uo"):
        smalls[nm + "_t"] = np.ascontiguousarray(full["L0_ffn_" + nm].T,
                                                 dtype=np.float32)

    in_maps = []
    for k in range(NCORES):
        l0 = k * LLOC
        m = dict(smalls)
        m["u"] = np.ascontiguousarray(uF[:, l0:l0 + LLOC, :])
        if k == 0:
            m["uh"] = np.zeros((B, 3, D), np.float32)
        else:
            m["uh"] = np.ascontiguousarray(uF[:, l0 - 3:l0, :])
        m2 = np.zeros((9, 8), np.float32)
        for j in range(8):
            for mm in range(8):
                if j < mm < k:
                    m2[mm + 1, j] = 1.0
            if not (j < k):
                m2[0, j] = -1e30
        m["m2t_ext"] = m2
        in_maps.append(m)
    return in_maps


_NC_CACHE = {}


def get_program(stage=99, debug=False):
    key = (stage, debug)
    if key not in _NC_CACHE:
        _NC_CACHE[key] = build_program(stage=stage, debug=debug)
    return _NC_CACHE[key]


def kernel(**inputs):
    in_maps = make_inputs(inputs)
    nc = get_program()
    res = run_bass_kernel_spmd(nc, in_maps, core_ids=list(range(NCORES)))
    outs = [res.results[k]["out"] for k in range(NCORES)]
    return np.concatenate(outs, axis=1)


if __name__ == "__main__":
    nc = build_program(stage=99, debug=False)
    print("built ok")
